# version 02
# baseline (speedup 1.0000x reference)
import sys
sys.path.insert(0, "/opt/trn_rl_repo")
import numpy as np
import concourse.bass as bass
import concourse.mybir as mybir
from concourse.bass_utils import run_bass_kernel_spmd




"""Raw-bass emission framework: per-engine streams + build-time semaphore counters.

Emission is kwargs-style (all APs/args concretized at emit time -> no late-binding
bugs). Sync policy: cross-engine RAW/WAW/WAR edges get producer.then_inc /
consumer wait_ge; one sem per producing engine gives a fence property (waiting a
later value subsumes earlier events on that engine). Same-engine plain streaming
TT/copy chains are trusted (probe-verified); memset/scan-initial/scalar-operand
same-engine chains get explicit edges.
"""

F32 = mybir.dt.float32
BF16 = mybir.dt.bfloat16
AF = mybir.ActivationFunctionType
ALU = mybir.AluOpType
AX = mybir.AxisListType

ENGINES = ("sync", "scalar", "vector", "tensor", "gpsimd")


class Sem:
    __slots__ = ("h", "n", "name")

    def __init__(self, h, name):
        self.h = h
        self.n = 0
        self.name = name


class Prog:
    def __init__(self, nc, stack):
        self.nc = nc
        self.stack = stack
        self.streams = {e: [] for e in ENGINES}
        self.ninstr = 0

    def sem(self, name):
        return Sem(self.stack.enter_context(self.nc.semaphore(name)), name)

    def sbuf(self, name, shape, dt):
        return self.stack.enter_context(self.nc.sbuf_tensor(name, shape, dt))

    def psum(self, name, shape, dt=F32):
        return self.stack.enter_context(self.nc.psum_tensor(name, shape, dt))

    def emit(self, eng, method, kwargs, wait=(), inc=None, inc_by=1):
        wait = [(s, v) for (s, v) in wait if s is not None]
        tgt = None
        if inc is not None:
            inc.n += inc_by
            tgt = inc.n
        self.streams[eng].append((method, kwargs, wait, inc, inc_by))
        self.ninstr += 1 + len(wait)
        return tgt

    def dma(self, eng, out, in_, sem, wait=(), **kw):
        return self.emit(eng, "dma_start", dict(out=out, in_=in_, **kw),
                         wait=wait, inc=sem, inc_by=16)

    def wait(self, eng, *pairs):
        self.streams[eng].append((None, None, [(s, v) for s, v in pairs], None, 0))
        self.ninstr += len(pairs)

    def finalize(self, block):
        for eng in ENGINES:
            ops = self.streams[eng]
            if not ops:
                continue
            section = getattr(block, eng)

            def body(engine, _ops=ops):
                for method, kwargs, wait, inc, inc_by in _ops:
                    for s, v in wait:
                        engine.wait_ge(s.h, v)
                    if method is None:
                        continue
                    ins = getattr(engine, method)(**kwargs)
                    if inc is not None:
                        ins.then_inc(inc.h, inc_by)

            section(body)
        self.streams = {e: [] for e in ENGINES}


def mm_group(P, out, pairs, wait=(), inc=None, first_extra=()):
    n = len(pairs)
    v = None
    for i, (l, r) in enumerate(pairs):
        v = P.emit("tensor", "matmul",
                   dict(out=out, lhsT=l, rhs=r, start=(i == 0), stop=(i == n - 1)),
                   wait=(list(wait) + list(first_extra)) if i == 0 else (),
                   inc=inc if i == n - 1 else None)
    return v


def rearr_kt(ap, p=128):
    return ap.rearrange("(kt p) o -> p kt o", p=p)


def sb3(tile, kt):
    return tile[:].rearrange("p (kt o) -> p kt o", kt=kt)


class PsumPool:

    def __init__(self):
        self.freed = {}

    def wait(self, t):
        f = self.freed.get(id(t))
        return [f] if f is not None else []

    def free(self, t, sem, val):
        self.freed[id(t)] = (sem, val)



"""Global buffer/sem allocation + prelude emission (constants, casts, router)."""

H, T, DIN, NST, NE, II = 1024, 4096, 2048, 16, 8, 2048
KTH, KTI = 8, 16


def declare_ffn_inputs(nc):
    d = {}
    b = {}
    shapes = {"su": (H, II), "sg": (H, II), "sd": (II, H)}
    for e in range(NE):
        shapes[f"up{e}"] = (H, II)
        shapes[f"gate{e}"] = (H, II)
        shapes[f"down{e}"] = (II, H)
    shapes["router"] = (H, 8)
    for name, shp in shapes.items():
        d[name] = nc.dram_tensor("f_" + name, list(shp), F32, kind="ExternalInput").ap()
        b[name] = nc.dram_tensor("b_" + name, list(shp), BF16, kind="Internal").ap()
    return d, b


def alloc_globals(P):
    g = {}
    g["pp"] = PsumPool()
    for s in ("s_x", "s_w", "s_wfree", "s_pe", "s_evA", "s_evV", "s_st", "s_cast",
              "s_pre", "s_sc", "s_evP"):
        g[s] = P.sem(s)
    g["mat_counter"] = 0
    g["PS"] = [P.psum(f"PS{i}", [128, 512]) for i in range(8)]
    g["slots"] = [P.sbuf(f"slot{i}", [128, 16384], BF16) for i in range(3)]
    g["xc32"] = P.sbuf("xc32", [128, 8192], F32)
    g["xn16"] = P.sbuf("xn16", [128, 8192], BF16)
    g["p16"] = P.sbuf("p16", [128, 8192], BF16)
    g["gs32"] = [P.sbuf(f"gs32_{i}", [128, 512], F32) for i in range(2)]
    g["sq32"] = g["gs32"]
    g["t32"] = [P.sbuf(f"t32_{i}", [128, 512], F32) for i in range(2)]
    wb1 = P.sbuf("wb32_0", [128, 512], F32)
    g["wb32"] = [wb1, wb1]
    g["rows"] = [P.sbuf(f"row{i}", [1, 512], F32) for i in range(3)]
    g["w_fm8"] = P.sbuf("w_fm8", [8, 1024], BF16)
    g["sel8"] = P.sbuf("sel8", [8, 1024], BF16)
    g["lg32"] = P.sbuf("lg32", [128, 64], F32)
    g["wr16"] = P.sbuf("wr16", [128, 64], BF16)
    g["bias_b32"] = P.sbuf("bias_b32", [128, 8], F32)
    g["ident32"] = P.sbuf("ident32", [128, 128], F32)
    g["ones_col32"] = P.sbuf("ones_col32", [128, 1], F32)
    g["ones_row32"] = P.sbuf("ones_row32", [1, 128], F32)
    g["ones16"] = P.sbuf("ones16", [1, 128], BF16)
    g["bias_row"] = P.sbuf("bias_row", [1, 8], F32)
    g["eps_col"] = P.sbuf("eps_col", [128, 1], F32)
    sm = {}
    for n, w in (("m12", 2), ("msk", 8), ("msked", 8), ("e12", 2), ("d", 1),
                 ("r", 1), ("t1", 1), ("w1", 1), ("w2", 1), ("w8", 8)):
        sm[n] = P.sbuf("sm_" + n, [128, w], F32)
    g["sm"] = sm
    return g


def emit_consts(P, g, nc):
    s_pre, s_evV = g["s_pre"], g["s_evV"]
    c_id = nc.dram_tensor("c_ident", [128, 128], F32, kind="ExternalInput").ap()
    c_ones = nc.dram_tensor("c_ones", [128, 1], F32, kind="ExternalInput").ap()
    c_ones_r = nc.dram_tensor("c_ones_r", [1, 128], F32, kind="ExternalInput").ap()
    P.dma("sync", g["ident32"][:], c_id, s_pre)
    P.dma("sync", g["ones_col32"][:], c_ones, s_pre)
    P.dma("sync", g["ones_r32_ld" if False else "ones_row32"][:], c_ones_r, s_pre)
    g["pre_consts"] = s_pre.n
    P.emit("vector", "memset", dict(ap=g["eps_col"][:], constant=1e-5), inc=s_evV)
    P.emit("vector", "memset", dict(ap=g["ones16"][:], constant=1.0), inc=s_evV)
    g["ones16_ready"] = g["s_evV"].n
    g["cast_val"] = {}


def emit_ffn_setup(P, g, nc, f_in, wbf, cast_order):
    s_cast, s_pre, s_pe, s_evV = g["s_cast"], g["s_pre"], g["s_pe"], g["s_evV"]
    c_bias = nc.dram_tensor("c_bias", [1, 8], F32, kind="ExternalInput").ap()
    c_sel8 = nc.dram_tensor("c_sel8", [8, 1024], BF16, kind="ExternalInput").ap()
    P.dma("sync", g["bias_row"][:], c_bias, s_pre)
    P.dma("sync", g["sel8"][:], c_sel8, s_pre)
    pre_ld = s_pre.n
    for name in cast_order:
        v = P.dma("gpsimd", wbf[name], f_in[name], s_cast)
        g["cast_val"][name] = v
    v = P.dma("sync", sb3(g["wr16"], KTH),
              wbf["router"].rearrange("(kt p) e -> p kt e", p=128), s_pre,
              wait=[(s_cast, g["cast_val"]["router"])])
    g["wr_ready"] = v
    t = P.emit("tensor", "matmul",
               dict(out=g["PS"][0][:, 0:8], lhsT=g["ones_row32"][0:1, :],
                    rhs=g["bias_row"][0:1, :], start=True, stop=True),
               wait=[(s_pre, pre_ld), (s_pre, g["pre_consts"])], inc=s_pe)
    v = P.emit("vector", "tensor_copy", dict(out=g["bias_b32"][:], in_=g["PS"][0][:, 0:8]),
               wait=[(s_pe, t)], inc=s_evV)
    g["pp"].free(g["PS"][0], s_evV, v)
    P.wait("tensor", (g["s_pre"], g["wr_ready"]))


def barrier(P, g):
    pairs = [(g[n], g[n].n) for n in ("s_x", "s_w", "s_pe", "s_evA", "s_evV",
                                      "s_st", "s_cast", "s_pre", "s_sc", "s_evP")]
    for eng in ("sync", "scalar", "vector", "tensor", "gpsimd"):
        P.wait(eng, *pairs)



"""FFN (MoE) phase emitter. Feature-major layout.

Every streamed matrix is [K, O] bf16, K*O = 2M elems -> slot [128, 16384] bf16.
lhsT for (kt, o0) = slot[:, kt*O + o0 : kt*O + o0 + 128].
"""


def emit_ln(P, g, *, tc, xsl, xn_dst, xw, pe_fence):
    KTH = 8
    NT = tc // 512
    pp = g["pp"]
    s_pe, s_evA, s_evV = g["s_pe"], g["s_evA"], g["s_evV"]
    PS, sq32, t32, rows = g["PS"], g["sq32"], g["t32"], g["rows"]
    xn_ready = None
    for nt in range(NT):
        pairs = [(g["ones_col32"][:, 0:1], xsl(kt, nt)) for kt in range(KTH)]
        mm_group(P, PS[6][0:1, :], pairs, wait=list(xw) + pp.wait(PS[6]))
        sq_done = []
        for kt in range(KTH):
            w = list(xw) + ([(s_pe, pe_fence)] if (nt == 0 and kt == 0) else [])
            if kt >= 2:
                w.append((s_pe, sq_done[kt - 2]))
            va = P.emit("scalar", "activation",
                        dict(out=sq32[kt % 2][:], in_=xsl(kt, nt), func=AF.Square),
                        wait=w, inc=s_evA)
            t = P.emit("tensor", "matmul",
                       dict(out=PS[7][0:1, :], lhsT=g["ones_col32"][:, 0:1],
                            rhs=sq32[kt % 2][:], start=(kt == 0), stop=(kt == KTH - 1)),
                       wait=[(s_evA, va)] + (pp.wait(PS[7]) if kt == 0 else []),
                       inc=s_pe)
            sq_done.append(t)
        vm = P.emit("vector", "tensor_scalar",
               dict(out=rows[0][0:1, :], in0=PS[6][0:1, :], scalar1=1.0 / 1024,
                    scalar2=None, op0=ALU.mult),
               wait=[(s_pe, sq_done[-1])], inc=s_evV)
        vms = P.emit("vector", "tensor_scalar",
                   dict(out=rows[1][0:1, :], in0=PS[7][0:1, :], scalar1=1.0 / 1024,
                        scalar2=None, op0=ALU.mult), inc=s_evV)
        pp.free(PS[6], s_evV, vms)
        pp.free(PS[7], s_evV, vms)
        v2 = P.emit("vector", "tensor_tensor",
               dict(out=rows[2][0:1, :], in0=rows[0][0:1, :], in1=rows[0][0:1, :], op=ALU.mult),
               wait=[(s_evV, vm)], inc=s_evV)
        vvar = P.emit("vector", "tensor_tensor",
                   dict(out=rows[2][0:1, :], in0=rows[1][0:1, :], in1=rows[2][0:1, :],
                        op=ALU.subtract), wait=[(s_evV, v2)], inc=s_evV)
        vln = P.emit("scalar", "activation",
                   dict(out=rows[2][0:1, :], in_=rows[2][0:1, :], func=AF.Ln,
                        bias=g["eps_col"][0:1, 0:1]),
                   wait=[(s_evV, vvar)], inc=s_evA)
        vexp = P.emit("scalar", "activation",
                   dict(out=rows[1][0:1, :], in_=rows[2][0:1, :], func=AF.Exp, scale=-0.5),
                   wait=[(s_evA, vln)], inc=s_evA)
        va = P.emit("vector", "scalar_tensor_tensor",
                    dict(out=rows[2][0:1, :], in0=rows[0][0:1, :], scalar=-1.0,
                         in1=rows[1][0:1, :], op0=ALU.mult, op1=ALU.mult),
                    wait=[(s_evA, vexp)], inc=s_evV)
        P.emit("tensor", "matmul",
               dict(out=PS[6][:], lhsT=g["ones_row32"][0:1, :], rhs=rows[1][0:1, :],
                    start=True, stop=True),
               wait=[(s_evV, va)] + pp.wait(PS[6]))
        tb = P.emit("tensor", "matmul",
                    dict(out=PS[7][:], lhsT=g["ones_row32"][0:1, :], rhs=rows[2][0:1, :],
                         start=True, stop=True),
                    wait=pp.wait(PS[7]), inc=s_pe)
        for kt in range(KTH):
            vt = P.emit("vector", "tensor_tensor",
                   dict(out=t32[kt % 2][:], in0=xsl(kt, nt), in1=PS[6][:], op=ALU.mult),
                   wait=[(s_pe, tb)] if kt == 0 else (), inc=s_evV)
            xn_ready = P.emit("vector", "tensor_tensor",
                              dict(out=xn_dst(kt, nt), in0=t32[kt % 2][:], in1=PS[7][:],
                                   op=ALU.add),
                              wait=[(s_evV, vt)], inc=s_evV)
        pp.free(PS[6], s_evV, xn_ready)
        pp.free(PS[7], s_evV, xn_ready)
    return xn_ready


def emit_ffn_phase(P, g, *, src_hbm, dst_hbm, n_experts=8, n_chunks=4, tc=1024,
                   chunk_list=None):
    KTH, KTI = 8, 16
    NT = tc // 512
    pp: PsumPool = g["pp"]
    s_x, s_w, s_wfree, s_pe, s_evA, s_evV, s_st, s_cast = (
        g["s_x"], g["s_w"], g["s_wfree"], g["s_pe"], g["s_evA"], g["s_evV"],
        g["s_st"], g["s_cast"])
    xc32, xn16, p16 = g["xc32"], g["xn16"], g["p16"]
    slots, PS = g["slots"], g["PS"]
    sq32, gs32, t32 = g["sq32"], g["gs32"], g["t32"]
    rows, wb32 = g["rows"], g["wb32"]
    w_fm8, sel8, lg32, sm = g["w_fm8"], g["sel8"], g["lg32"], g["sm"]
    if chunk_list is None:
        chunk_list = list(range(n_chunks))

    mat_names = ["su", "sg", "sd"]
    for e in range(n_experts):
        mat_names += [f"up{e}", f"gate{e}", f"down{e}"]

    w_avail = {}

    def emit_chunk_weight_loads(c):
        for name in mat_names:
            m = g["mat_counter"]
            waits = [(s_wfree, m - 2)] if m >= 3 else []
            waits.append((s_cast, g["cast_val"][name]))
            hbm = g["wbf"][name]
            kt = hbm.shape[0] // 128
            v = P.dma("sync", sb3(slots[m % 3], kt), rearr_kt(hbm), s_w, wait=waits)
            w_avail[(c, name)] = (v, slots[m % 3])
            g["mat_counter"] += 1

    src_r = src_hbm.rearrange("(kt p) t -> p kt t", p=128)
    dst_r = dst_hbm.rearrange("(kt p) t -> p kt t", p=128)
    for c in chunk_list:
        emit_chunk_weight_loads(c)
        pe_fence = s_pe.n
        vld = P.dma("gpsimd",
                    xc32[:, 0:KTH * tc].rearrange("p (kt t) -> p kt t", kt=KTH),
                    src_r[:, :, c * tc:(c + 1) * tc], s_x,
                    wait=[(s_st, s_st.n), (s_evA, s_evA.n)])
        xw = [(s_x, vld)]

        def xsl(kt, nt, c=c):
            return xc32[:, kt * tc + nt * 512: kt * tc + nt * 512 + 512]

        def xnd(kt, nt, c=c):
            return xn16[:, kt * tc + nt * 512: kt * tc + nt * 512 + 512]
        xn_ready = emit_ln(P, g, tc=tc, xsl=xsl, xn_dst=xnd, xw=xw, pe_fence=pe_fence)

        for tt in range(tc // 128):
            pt = PS[4 + (tt % 2)]
            pairs = [(xn16[:, kt * tc + tt * 128: kt * tc + tt * 128 + 128],
                      g["wr16"][:, kt * 8:(kt + 1) * 8]) for kt in range(KTH)]
            tl = mm_group(P, pt[:, 0:8], pairs,
                          wait=([(s_evV, xn_ready)] if tt == 0 else []) + pp.wait(pt),
                          inc=s_pe)
            lsl = lg32[:, tt * 8:(tt + 1) * 8]
            v = P.emit("vector", "tensor_tensor",
                       dict(out=lsl, in0=pt[:, 0:8], in1=g["bias_b32"][:, 0:8], op=ALU.add),
                       wait=[(s_pe, tl)], inc=s_evV)
            pp.free(pt, s_evV, v)
            m12, msk, w8 = sm["m12"], sm["msk"], sm["w8"]
            vlsl = v
            v = P.emit("vector", "tensor_reduce",
                       dict(out=m12[:, 0:1], in_=lsl, axis=AX.X, op=ALU.max),
                       wait=[(s_evV, vlsl)], inc=s_evV)
            vmsk1 = P.emit("vector", "tensor_scalar",
                   dict(out=msk[:], in0=lsl, scalar1=m12[:, 0:1], scalar2=None,
                        op0=ALU.is_equal), wait=[(s_evV, v)], inc=s_evV)
            vq = P.emit("vector", "scalar_tensor_tensor",
                   dict(out=sm["msked"][:], in0=msk[:], scalar=-1e30, in1=lsl,
                        op0=ALU.mult, op1=ALU.add), wait=[(s_evV, vmsk1)], inc=s_evV)
            v = P.emit("vector", "tensor_reduce",
                       dict(out=m12[:, 1:2], in_=sm["msked"][:], axis=AX.X, op=ALU.max),
                       wait=[(s_evV, vq)], inc=s_evV)
            v = P.emit("scalar", "activation",
                       dict(out=sm["e12"][:], in_=m12[:], func=AF.Exp, scale=-1.0),
                       wait=[(s_evV, v)], inc=s_evA)
            v = P.emit("vector", "tensor_tensor",
                   dict(out=sm["d"][:], in0=sm["e12"][:, 0:1], in1=sm["e12"][:, 1:2],
                        op=ALU.add), wait=[(s_evA, v)], inc=s_evV)
            v = P.emit("vector", "tensor_scalar",
                       dict(out=sm["d"][:], in0=sm["d"][:], scalar1=2.0, scalar2=None,
                            op0=ALU.add), wait=[(s_evV, v)], inc=s_evV)
            v = P.emit("vector", "reciprocal", dict(out=sm["r"][:], in_=sm["d"][:]),
                   wait=[(s_evV, v)], inc=s_evV)
            v = P.emit("vector", "tensor_scalar",
                       dict(out=sm["r"][:], in0=sm["r"][:], scalar1=0.5, scalar2=None,
                            op0=ALU.mult), wait=[(s_evV, v)], inc=s_evV)
            vt1 = P.emit("vector", "tensor_scalar",
                   dict(out=sm["t1"][:], in0=sm["e12"][:, 1:2], scalar1=1.0, scalar2=None,
                        op0=ALU.add), wait=[(s_evV, v)], inc=s_evV)
            vw1 = P.emit("vector", "tensor_tensor",
                   dict(out=sm["w1"][:], in0=sm["t1"][:], in1=sm["r"][:], op=ALU.mult),
                   wait=[(s_evV, vt1)], inc=s_evV)
            vt1 = P.emit("vector", "tensor_scalar",
                   dict(out=sm["t1"][:], in0=sm["e12"][:, 0:1], scalar1=1.0, scalar2=None,
                        op0=ALU.add), wait=[(s_evV, vw1)], inc=s_evV)
            v = P.emit("vector", "tensor_tensor",
                       dict(out=sm["w2"][:], in0=sm["t1"][:], in1=sm["r"][:], op=ALU.mult),
                       wait=[(s_evV, vt1)], inc=s_evV)
            vw8 = P.emit("vector", "tensor_scalar",
                   dict(out=w8[:], in0=msk[:], scalar1=sm["w1"][:, 0:1], scalar2=None,
                        op0=ALU.mult), wait=[(s_evV, v), (s_evV, vmsk1)], inc=s_evV)
            vmsk2 = P.emit("vector", "tensor_scalar",
                   dict(out=msk[:], in0=lsl, scalar1=m12[:, 1:2], scalar2=None,
                        op0=ALU.is_equal), wait=[(s_evV, vw8)], inc=s_evV)
            v = P.emit("vector", "scalar_tensor_tensor",
                       dict(out=w8[:], in0=msk[:], scalar=sm["w2"][:, 0:1], in1=w8[:],
                            op0=ALU.mult, op1=ALU.add), wait=[(s_evV, vmsk2)], inc=s_evV)
            ptp = PS[4 + ((tt + 1) % 2)]
            t2 = P.emit("tensor", "transpose",
                        dict(out=ptp[0:8, 0:128], in_=w8[:], identity=g["ident32"][:]),
                        wait=[(s_evV, v)] + pp.wait(ptp), inc=s_pe)
            v = P.emit("vector", "tensor_copy",
                       dict(out=w_fm8[:, tt * 128:(tt + 1) * 128], in_=ptp[0:8, 0:128]),
                       wait=[(s_pe, t2)], inc=s_evV)
            pp.free(ptp, s_evV, v)
        v_wr = v

        gs_last = {0: None, 1: None}
        tD_last = None
        for ei in range(-1, n_experts):
            names = ("su", "sg", "sd") if ei < 0 else (f"up{ei}", f"gate{ei}", f"down{ei}")
            (vU, slot_u), (vG, slot_g), (vD, slot_d) = (w_avail[(c, n)] for n in names)
            for nt in range(NT):
                if ei >= 0:
                    pw = PS[6 + (nt % 2)]
                    twb = P.emit("tensor", "matmul",
                                 dict(out=pw[:], lhsT=sel8[:, ei * 128:(ei + 1) * 128],
                                      rhs=w_fm8[0:8, nt * 512: nt * 512 + 512],
                                      start=True, stop=True),
                                 wait=pp.wait(pw) + [(s_evV, v_wr)], inc=s_pe)
                    vwb = P.emit("vector", "tensor_copy",
                                 dict(out=wb32[nt % 2][:], in_=pw[:]),
                                 wait=[(s_pe, twb)], inc=s_evV)
                    pp.free(pw, s_evV, vwb)
                    my_wb = wb32[nt % 2]
                else:
                    my_wb = None
                vp = None
                for ot in range(KTI):
                    pu, pg = PS[0 + (ot % 2)], PS[2 + (ot % 2)]
                    w0 = pp.wait(pu) + ([(s_w, vU)] if (nt == 0 and ot == 0) else [])
                    pairs_u = [(slot_u[:, kt * 2048 + ot * 128: kt * 2048 + ot * 128 + 128],
                                xn16[:, kt * tc + nt * 512: kt * tc + nt * 512 + 512])
                               for kt in range(KTH)]
                    tU = mm_group(P, pu[:], pairs_u, wait=w0, inc=s_pe)
                    pairs_g = [(slot_g[:, kt * 2048 + ot * 128: kt * 2048 + ot * 128 + 128],
                                xn16[:, kt * tc + nt * 512: kt * tc + nt * 512 + 512])
                               for kt in range(KTH)]
                    tG = mm_group(P, pg[:], pairs_g,
                                  wait=pp.wait(pg) + ([(s_w, vG)] if (nt == 0 and ot == 0) else []),
                                  inc=s_pe)
                    wA = [(s_pe, tG)]
                    if gs_last[ot % 2] is not None:
                        wA.append((s_evV, gs_last[ot % 2]))
                    vg = P.emit("scalar", "activation",
                                dict(out=gs32[ot % 2][:], in_=pg[:], func=AF.Silu),
                                wait=wA, inc=s_evA)
                    pp.free(pg, s_evA, vg)
                    vp = P.emit("vector", "tensor_tensor",
                                dict(out=p16[:, ot * 512:(ot + 1) * 512],
                                     in0=gs32[ot % 2][:], in1=pu[:], op=ALU.mult),
                                wait=[(s_evA, vg), (s_pe, tU)] +
                                     ([(s_pe, tD_last)] if (ot == 0 and tD_last is not None) else []),
                                inc=s_evV)
                    pp.free(pu, s_evV, vp)
                    gs_last[ot % 2] = vp
                if nt == NT - 1:
                    P.wait("tensor", (s_pe, s_pe.n))
                    P.emit("tensor", "nop", {}, inc=s_wfree)
                    P.emit("tensor", "nop", {}, inc=s_wfree)
                for oh in range(KTH):
                    pd = PS[4 + (oh % 2)]
                    pairs_d = [(slot_d[:, kt * 1024 + oh * 128: kt * 1024 + oh * 128 + 128],
                                p16[:, kt * 512:(kt + 1) * 512]) for kt in range(KTI)]
                    tD = mm_group(P, pd[:], pairs_d,
                                  wait=pp.wait(pd) + [(s_evV, vp)] +
                                       ([(s_w, vD)] if (nt == 0 and oh == 0) else []),
                                  inc=s_pe)
                    if my_wb is None:
                        vd = P.emit("vector", "tensor_scalar",
                                    dict(out=t32[oh % 2][:], in0=pd[:], scalar1=0.5,
                                         scalar2=None, op0=ALU.mult),
                                    wait=[(s_pe, tD)], inc=s_evV)
                    else:
                        vd = P.emit("vector", "tensor_tensor",
                                    dict(out=t32[oh % 2][:], in0=pd[:], in1=my_wb[:],
                                         op=ALU.mult),
                                    wait=[(s_pe, tD)], inc=s_evV)
                    pp.free(pd, s_evV, vd)
                    vcomb = P.emit("vector", "tensor_tensor",
                           dict(out=xc32[:, oh * tc + nt * 512: oh * tc + nt * 512 + 512],
                                in0=xc32[:, oh * tc + nt * 512: oh * tc + nt * 512 + 512],
                                in1=t32[oh % 2][:], op=ALU.add),
                           wait=[(s_evV, vd)], inc=s_evV)
                    tD_last = tD
            P.wait("tensor", (s_pe, s_pe.n))
            P.emit("tensor", "nop", {}, inc=s_wfree)

        P.dma("gpsimd", dst_r[:, :, c * tc:(c + 1) * tc],
              xc32[:, 0:KTH * tc].rearrange("p (kt t) -> p kt t", kt=KTH),
              s_st, wait=[(s_evV, vcomb)])



"""Mamba phase emitters (P2a: in_proj/conv/silu/z/x_proj; P2b: dt/scan/y/out_proj).

Feature-major. Per core: T=4096 tokens, D_INNER=2048 (16 dtiles), D_STATE=16.
Chunk = 512 tokens (8 chunks).

Host layouts:
  f_winx/f_winz [1024, 2048] f32 = in_proj_w.T halves; f_wout [2048, 1024] = out_proj_w.T
  f_xp [2048, 33] = x_proj_w.T
  c_dtw/c_dtb/c_Dd/c_convb [128, 16] f32: (p, j) = vec[j*128 + p]
  c_convw [128, 64]: (p, 4*j+k) = conv_w[j*128+p, 0, k]
  c_alog [128, 256]: (p, j*16+n) = A_log[j*128+p, n]
  c_selxp [33, 33*128] bf16: block j = [33,128] with row j all-ones (selection lhsT)

Weight slots: winx/winz/wout occupy ring slots 0/1/2 as ring matrices (mat_counter
advanced by 3; s_wfree incremented at their last use so the FFN phase 2 ring
streaming stays consistent).
"""

KTH, KTI, NST = 8, 16, 16
TC2 = 512
NCH2 = 8


def declare_p2(P, g, nc):
    g["xssm"] = [P.sbuf(f"xssm{i}", [128, 515], F32) for i in range(2)]
    g["carry32"] = P.sbuf("carry32", [128, 48], F32)
    g["xps16"] = P.sbuf("xps16", [33, 512], BF16)
    g["xrow"] = [P.sbuf(f"xrow{i}", [1, 512], BF16) for i in range(2)]
    g["xrow2"] = [P.sbuf(f"xrow2_{i}", [1, 512], BF16) for i in range(2)]
    g["A_neg"] = P.sbuf("A_neg", [128, 256], F32)
    g["dtw32"] = P.sbuf("dtw32", [128, 16], F32)
    g["dtb32"] = P.sbuf("dtb32", [128, 16], F32)
    g["Dd32"] = P.sbuf("Dd32", [128, 16], F32)
    g["convw32"] = P.sbuf("convw32", [128, 64], F32)
    g["convb32"] = P.sbuf("convb32", [128, 16], F32)
    g["wxp16"] = P.sbuf("wxp16", [128, 16 * 33], BF16)
    g["hstate"] = P.sbuf("hstate", [128, 256], BF16)
    g["dAb"] = [P.sbuf(f"dAb{i}", [128, 512], BF16) for i in range(2)]
    g["dBxb"] = [P.sbuf(f"dBxb{i}", [128, 512], BF16) for i in range(2)]
    g["hb"] = [P.sbuf(f"hb{i}", [128, 512], BF16) for i in range(2)]
    g["Bv16"] = [P.sbuf(f"Bv16_{i}", [128, 512], BF16) for i in range(2)]
    g["Cv16"] = [P.sbuf(f"Cv16_{i}", [128, 512], BF16) for i in range(2)]
    g["szst"] = [P.sbuf(f"szst{i}", [128, 512], BF16) for i in range(2)]
    g["yst"] = g["szst"]
    d, b = {}, {}
    for name, shp in (("winx", (1024, 2048)), ("winz", (1024, 2048)),
                      ("wout", (2048, 1024)), ("xp", (2048, 33))):
        d[name] = nc.dram_tensor("f_" + name, list(shp), F32, kind="ExternalInput").ap()
        b[name] = nc.dram_tensor("b_" + name, list(shp), BF16, kind="Internal").ap()
    for name, shp in (("dtw", (128, 16)), ("dtb", (128, 16)), ("Dd", (128, 16)),
                      ("convw", (128, 64)), ("convb", (128, 16)), ("alog", (128, 256))):
        d[name] = nc.dram_tensor("c_" + name, list(shp), F32, kind="ExternalInput").ap()
    g["m_in"], g["m_bf"] = d, b
    g["xc_hbm"] = nc.dram_tensor("xc_hbm", [2048, 4096], F32, kind="Internal").ap()
    g["sz_hbm"] = nc.dram_tensor("sz_hbm", [2048, 4096], BF16, kind="Internal").ap()
    g["xp_hbm"] = nc.dram_tensor("xp_hbm", [33, 4096], BF16, kind="Internal").ap()


def emit_p2_casts(P, g):
    d, b = g["m_in"], g["m_bf"]
    for name in ("winx", "winz", "wout", "xp"):
        g["cast_val"][name] = P.dma("gpsimd", b[name], d[name], g["s_cast"])


def emit_p2_prelude(P, g):
    d = g["m_in"]
    s_pre, s_evA, s_evV, s_cast = g["s_pre"], g["s_evA"], g["s_evV"], g["s_cast"]
    for name, dst in (("dtw", "dtw32"), ("dtb", "dtb32"), ("Dd", "Dd32"),
                      ("convw", "convw32"), ("convb", "convb32")):
        P.dma("sync", g[dst][:], d[name], s_pre)
    v = P.dma("sync", g["A_neg"][:], d["alog"], s_pre)
    va = P.emit("scalar", "activation",
                dict(out=g["A_neg"][:], in_=g["A_neg"][:], func=AF.Exp),
                wait=[(s_pre, v)], inc=s_evA)
    g["A_ready"] = P.emit("vector", "tensor_scalar",
                          dict(out=g["A_neg"][:], in0=g["A_neg"][:], scalar1=-1.0,
                               scalar2=None, op0=ALU.mult),
                          wait=[(s_evA, va)], inc=s_evV)
    g["wxp_ready"] = P.dma("sync", sb3(g["wxp16"], KTI),
                           g["m_bf"]["xp"].rearrange("(kt p) o -> p kt o", p=128),
                           s_pre, wait=[(s_cast, g["cast_val"]["xp"])])
    g["carry_ready"] = P.emit("vector", "memset",
                              dict(ap=g["carry32"][:], constant=0.0), inc=s_evV)
    g["h_ready"] = P.emit("vector", "memset",
                          dict(ap=g["hstate"][:], constant=0.0), inc=s_evV)


def emit_p2a(P, g, *, src_hbm):
    pp = g["pp"]
    s_x, s_w, s_wfree, s_pe, s_evA, s_evV, s_sc, s_cast, s_pre = (
        g["s_x"], g["s_w"], g["s_wfree"], g["s_pe"], g["s_evA"], g["s_evV"],
        g["s_sc"], g["s_cast"], g["s_pre"])
    PS, slots = g["PS"], g["slots"]
    xch32 = g["p16"][:].bitcast(F32)
    xn16, xc32 = g["xn16"], g["xc32"]
    carry = g["carry32"]
    t32 = g["t32"]
    zs16 = [t[:].bitcast(BF16) for t in g["wb32"]]
    xc16 = [t[:].bitcast(BF16) for t in g["gs32"]]
    src_r = src_hbm.rearrange("(kt p) t -> p kt t", p=128)
    xc_r = g["xc_hbm"].rearrange("(kt p) t -> p kt t", p=128)
    sz = g["sz_hbm"]

    m0 = g["mat_counter"]
    g["mat_counter"] += 3
    vwx = P.dma("sync", sb3(slots[m0 % 3], KTH), rearr_kt(g["m_bf"]["winx"]), s_w,
                wait=([(s_wfree, m0 - 2)] if m0 >= 3 else []) +
                     [(s_cast, g["cast_val"]["winx"])])
    vwz = P.dma("sync", sb3(slots[(m0 + 1) % 3], KTH), rearr_kt(g["m_bf"]["winz"]), s_w,
                wait=([(s_wfree, m0 - 1)] if m0 + 1 >= 3 else []) +
                     [(s_cast, g["cast_val"]["winz"])])
    vwo = P.dma("sync", sb3(slots[(m0 + 2) % 3], KTI), rearr_kt(g["m_bf"]["wout"]), s_w,
                wait=([(s_wfree, m0)] if m0 + 2 >= 3 else []) +
                     [(s_cast, g["cast_val"]["wout"])])
    g["wout_ready"] = vwo
    slot_wx, slot_wz, g["slot_wo"] = slots[m0 % 3], slots[(m0 + 1) % 3], slots[(m0 + 2) % 3]

    xssm_rd = {0: None, 1: None}
    silu_v = {0: None, 1: None}
    xc16_v = {0: None, 1: None}
    zdma_v = {0: None, 1: None}
    g["xc_store"] = {}
    for c in range(NCH2):
        pe_fence = s_pe.n
        vx = P.dma("gpsimd", xch32[:, 0:4096].rearrange("p (kt t) -> p kt t", kt=KTH),
                   src_r[:, :, c * TC2:(c + 1) * TC2], s_x,
                   wait=[(s_evA, s_evA.n), (s_pe, s_pe.n), (s_evV, s_evV.n)])
        xw = [(s_x, vx)]

        def xsl(kt, nt, c=c):
            return xch32[:, kt * TC2: kt * TC2 + 512]

        def xnd(kt, nt, c=c):
            return xn16[:, kt * TC2: kt * TC2 + 512]
        xn_ready = emit_ln(P, g, tc=TC2, xsl=xsl, xn_dst=xnd, xw=xw, pe_fence=pe_fence)

        xp_stop = None
        last_silu = None
        for i in range(KTI):
            px = PS[0 + (i % 2)]
            pairs = [(slot_wx[:, kt * 2048 + i * 128: kt * 2048 + i * 128 + 128],
                      xn16[:, kt * TC2: kt * TC2 + 512]) for kt in range(KTH)]
            tX = mm_group(P, px[:], pairs,
                          wait=pp.wait(px) +
                               ([(s_w, vwx), (s_evV, xn_ready)] if i == 0 else []),
                          inc=s_pe)
            xt = g["xssm"][i % 2]
            wA = [(s_pe, tX)]
            if xssm_rd[i % 2] is not None:
                wA.append((s_evV, xssm_rd[i % 2]))
            vev = P.emit("scalar", "copy", dict(out=xt[:, 3:515], in_=px[:]),
                         wait=wA, inc=s_evA)
            pp.free(px, s_evA, vev)
            vcr = P.emit("vector", "tensor_copy",
                         dict(out=xt[:, 0:3], in_=carry[:, i * 3:i * 3 + 3]),
                         wait=([(s_evV, g["carry_ready"])] if (c == 0 and i < 3) else []) +
                              [(s_evA, vev)],
                         inc=s_evV)
            wT = [(s_evV, vcr)]
            if silu_v[i % 2] is not None:
                wT.append((s_evA, silu_v[i % 2]))
            vt = P.emit("vector", "tensor_scalar",
                        dict(out=t32[i % 2][:], in0=xt[:, 0:512],
                             scalar1=g["convw32"][:, 4 * i:4 * i + 1], scalar2=None,
                             op0=ALU.mult),
                        wait=wT, inc=s_evV)
            for k in range(1, 4):
                vt = P.emit("vector", "scalar_tensor_tensor",
                            dict(out=t32[i % 2][:], in0=xt[:, k:k + 512],
                                 scalar=g["convw32"][:, 4 * i + k:4 * i + k + 1],
                                 in1=t32[i % 2][:], op0=ALU.mult, op1=ALU.add),
                            wait=[(s_evV, vt)], inc=s_evV)
            vco = P.emit("vector", "tensor_copy",
                         dict(out=carry[:, i * 3:i * 3 + 3], in_=xt[:, 512:515]),
                         inc=s_evV)
            xssm_rd[i % 2] = vco
            wS = [(s_evV, vt)]
            if c > 0 and i == 0:
                wS.append((s_sc, g["xc_store"][c - 1]))
            last_silu = P.emit("scalar", "activation",
                               dict(out=xc32[:, i * TC2:(i + 1) * TC2], in_=t32[i % 2][:],
                                    func=AF.Silu, bias=g["convb32"][:, i:i + 1]),
                               wait=wS, inc=s_evA)
            silu_v[i % 2] = last_silu
            vxc = P.emit("vector", "tensor_copy",
                         dict(out=xc16[i % 2][:, 0:512],
                              in_=xc32[:, i * TC2:(i + 1) * TC2]),
                         wait=[(s_evA, last_silu)] +
                              ([(s_evV, xc16_v[i % 2])] if xc16_v[i % 2] else []),
                         inc=s_evV)
            xp_stop = P.emit("tensor", "matmul",
                             dict(out=PS[4][0:33, 0:512],
                                  lhsT=g["wxp16"][:, i * 33:(i + 1) * 33],
                                  rhs=xc16[i % 2][:, 0:512],
                                  start=(i == 0), stop=(i == KTI - 1),
                                  skip_group_check=True),
                             wait=[(s_evV, vxc)] +
                                  ([(s_pre, g["wxp_ready"])] if i == 0 else []) +
                                  (pp.wait(PS[4]) if i == 0 else []),
                             inc=s_pe)
            xc16_v[i % 2] = xp_stop
            pz = PS[2 + (i % 2)]
            pairs = [(slot_wz[:, kt * 2048 + i * 128: kt * 2048 + i * 128 + 128],
                      xn16[:, kt * TC2: kt * TC2 + 512]) for kt in range(KTH)]
            tZ = mm_group(P, pz[:], pairs,
                          wait=pp.wait(pz) + ([(s_w, vwz)] if i == 0 else []),
                          inc=s_pe)
            wZ = [(s_pe, tZ)]
            if zdma_v[i % 2] is not None:
                wZ.append((s_sc, zdma_v[i % 2]))
            vz = P.emit("scalar", "activation",
                        dict(out=zs16[i % 2][:, 0:512], in_=pz[:], func=AF.Silu),
                        wait=wZ, inc=s_evA)
            pp.free(pz, s_evA, vz)
            zdma_v[i % 2] = P.dma("scalar",
                                  sz[i * 128:(i + 1) * 128, c * TC2:(c + 1) * TC2],
                                  zs16[i % 2][:, 0:512], s_sc, wait=[(s_evA, vz)])
        wxpv = [(s_pe, xp_stop)]
        if c > 0:
            wxpv.append((s_sc, g["xp_store"][c - 1]))
        vxp = P.emit("vector", "tensor_copy",
                     dict(out=g["xps16"][0:33, 0:512], in_=PS[4][0:33, 0:512]),
                     wait=wxpv, inc=s_evV)
        pp.free(PS[4], s_evV, vxp)
        g.setdefault("xp_store", {})[c] = P.dma(
            "scalar", g["xp_hbm"][:, c * TC2:(c + 1) * TC2], g["xps16"][0:33, 0:512],
            s_sc, wait=[(s_evV, vxp)])
        g["xc_store"][c] = P.dma(
            "scalar", xc_r[:, :, c * TC2:(c + 1) * TC2],
            xc32[:, 0:KTI * TC2].rearrange("p (kt t) -> p kt t", kt=KTI),
            s_sc, wait=[(s_evA, last_silu)])
    P.wait("tensor", (s_pe, s_pe.n))
    P.emit("tensor", "nop", {}, inc=s_wfree)
    P.emit("tensor", "nop", {}, inc=s_wfree)


def emit_p2b(P, g, *, src_hbm, dst_hbm):
    pp = g["pp"]
    s_x, s_w, s_wfree, s_pe, s_evA, s_evV, s_evP, s_sc, s_st = (
        g["s_x"], g["s_w"], g["s_wfree"], g["s_pe"], g["s_evA"], g["s_evV"],
        g["s_evP"], g["s_sc"], g["s_st"])
    PS = g["PS"]
    slot_wo = g["slot_wo"]
    dt16 = g["p16"]
    xch32 = g["p16"][:].bitcast(F32)
    u16 = g["xn16"]
    xc32 = g["xc32"]
    t32 = g["t32"]
    xrow = g["xrow"]
    dA, dBx, hb = g["dAb"], g["dBxb"], g["hb"]
    Bv, Cv = g["Bv16"], g["Cv16"]
    hstate = g["hstate"]
    src_r = src_hbm.rearrange("(kt p) t -> p kt t", p=128)
    dst_r = dst_hbm.rearrange("(kt p) t -> p kt t", p=128)
    xc_r = g["xc_hbm"].rearrange("(kt p) t -> p kt t", p=128)
    sz = g["sz_hbm"]

    rowv_g = {}
    rowv_pe = {}
    for c in range(NCH2):
        cs = slice(c * TC2, (c + 1) * TC2)
        vxc = P.dma("scalar",
                    xc32[:, 0:KTI * TC2].rearrange("p (kt t) -> p kt t", kt=KTI),
                    xc_r[:, :, cs], s_sc,
                    wait=[(s_sc, g["xc_store"][c]), (s_evV, s_evV.n),
                          (s_evP, s_evP.n), (s_evA, s_evA.n)])
        vdtr = P.dma("scalar", g["xps16"][0:1, 0:512], g["xp_hbm"][0:1, cs], s_sc,
                     wait=[(s_sc, g["xp_store"][c]), (s_pe, s_pe.n),
                           (g["s_evV"], g["s_evV"].n)])
        rdtr = P.emit("vector", "tensor_copy",
                      dict(out=g["xrow2"][0][0:1, :], in_=g["xps16"][0:1, 0:512]),
                      wait=[(s_sc, vdtr)] +
                           ([(s_pe, rowv_pe[c * NST - 1])] if (c * NST - 1) in rowv_pe
                            else []),
                      inc=s_evV)
        tdtr = P.emit("tensor", "matmul",
                      dict(out=PS[5][:], lhsT=g["ones16"][0:1, :],
                           rhs=g["xrow2"][0][0:1, :], start=True, stop=True),
                      wait=pp.wait(PS[5]) + [(s_evV, rdtr),
                                             (g["s_evV"], g["ones16_ready"])],
                      inc=s_pe)
        rowv_pe[c * NST - 1] = tdtr
        vdt = {}
        prevdt = None
        for i in range(KTI):
            v0 = P.emit("vector", "tensor_scalar",
                        dict(out=t32[0][:], in0=PS[5][:],
                             scalar1=g["dtw32"][:, i:i + 1],
                             scalar2=g["dtb32"][:, i:i + 1],
                             op0=ALU.mult, op1=ALU.add),
                        wait=[(s_pe, tdtr)] +
                             ([(s_evV, prevdt)] if prevdt is not None else []),
                        inc=s_evV)
            v1 = P.emit("scalar", "activation",
                        dict(out=t32[1][:], in_=t32[0][:], func=AF.Abs),
                        wait=[(s_evV, v0)], inc=s_evA)
            v2 = P.emit("scalar", "activation",
                        dict(out=t32[1][:], in_=t32[1][:], func=AF.Exp, scale=-1.0),
                        wait=[(s_evA, v1)], inc=s_evA)
            v3 = P.emit("scalar", "activation",
                        dict(out=t32[1][:], in_=t32[1][:], func=AF.Ln, bias=1.0),
                        wait=[(s_evA, v2)], inc=s_evA)
            v4 = P.emit("vector", "tensor_scalar",
                        dict(out=t32[0][:], in0=t32[0][:], scalar1=0.0, scalar2=None,
                             op0=ALU.max),
                        wait=[(s_evA, v1)], inc=s_evV)
            v5 = P.emit("vector", "tensor_tensor",
                        dict(out=dt16[:, i * TC2:(i + 1) * TC2], in0=t32[0][:],
                             in1=t32[1][:], op=ALU.add),
                        wait=[(s_evV, v4), (s_evA, v3)] +
                             ([(s_st, s_st.n)] if i == 0 else []), inc=s_evV)
            vdt[i] = v5
            prevdt = v5
        pp.free(PS[5], s_evV, vdt[KTI - 1])
        vu, vyb = {}, {}
        for i in range(KTI):
            vu[i] = P.emit("vector", "tensor_tensor",
                           dict(out=u16[:, i * TC2:(i + 1) * TC2],
                                in0=dt16[:, i * TC2:(i + 1) * TC2],
                                in1=xc32[:, i * TC2:(i + 1) * TC2], op=ALU.mult),
                           wait=[(s_evV, vdt[i]), (s_sc, vxc)], inc=s_evV)
        for i in range(KTI):
            vyb[i] = P.emit("vector", "tensor_scalar",
                            dict(out=xc32[:, i * TC2:(i + 1) * TC2],
                                 in0=xc32[:, i * TC2:(i + 1) * TC2],
                                 scalar1=g["Dd32"][:, i:i + 1], scalar2=None,
                                 op0=ALU.mult),
                            wait=[(s_evV, vu[i])], inc=s_evV)
        scan_v, ymul_v, hst_v, dbx_v, da_v, yadd_v = {}, {}, {}, {}, {}, {}
        yadd_sem = s_evV if g.get("yadd_dve") else s_evP
        ymul_guard = {0: (yadd_sem, yadd_sem.n), 1: (yadd_sem, yadd_sem.n)}

        def emit_tail(i):
            n_, dt_ = divmod(i, KTI)
            ib = i % 2
            col = dt_ * NST + n_
            sv = P.emit("vector", "tensor_tensor_scan",
                        dict(out=hb[ib][:], data0=dA[ib][:], data1=dBx[ib][:],
                             initial=hstate[:, col:col + 1],
                             op0=ALU.mult, op1=ALU.add),
                        wait=[(s_evA, da_v[i]), (s_evV, dbx_v[i])] +
                             ([(s_evV, hst_v[i - 2])] if i >= 2 else
                              ([(s_evV, g["h_ready"])] if c == 0 else [])),
                        inc=s_evV)
            scan_v[i] = sv
            mv = P.emit("vector", "tensor_tensor",
                        dict(out=g["yst"][i % 2][:], in0=hb[ib][:], in1=Cv[n_ % 2][:],
                             op=ALU.mult),
                        wait=[(s_evV, sv), ymul_guard[i % 2]],
                        inc=s_evV)
            ymul_v[i] = mv
            av = P.emit("vector" if g.get("yadd_dve") else "gpsimd", "tensor_tensor",
                        dict(out=xc32[:, dt_ * TC2:(dt_ + 1) * TC2],
                             in0=xc32[:, dt_ * TC2:(dt_ + 1) * TC2],
                             in1=g["yst"][i % 2][:], op=ALU.add),
                        wait=[(s_evV, mv), (s_evV, vyb[dt_])],
                        inc=yadd_sem)
            yadd_v[dt_] = av
            ymul_guard[i % 2] = (yadd_sem, av)
            hst_v[i] = P.emit("vector", "tensor_copy",
                              dict(out=hstate[:, col:col + 1], in_=hb[ib][:, 511:512]),
                              wait=[(s_evV, sv)], inc=s_evV)

        rowv = rowv_g
        for n in range(NST):
            key = c * NST + n
            rg = [(s_evV, rowv[key - 1])] if (key - 1) in rowv else []
            vrB = P.dma("scalar", xrow[0][0:1, :],
                        g["xp_hbm"][1 + n:2 + n, cs], s_sc,
                        wait=rg + ([(s_sc, g["xp_store"][c])] if n == 0 else []))
            vrC = P.dma("scalar", xrow[1][0:1, :],
                        g["xp_hbm"][17 + n:18 + n, cs], s_sc)
            w2 = [(s_pe, rowv_pe[key - 1])] if (key - 1) in rowv_pe else []
            rvB = P.emit("vector", "tensor_copy",
                         dict(out=g["xrow2"][0][0:1, :], in_=xrow[0][0:1, :]),
                         wait=[(s_sc, vrB)] + w2, inc=s_evV)
            rvC = P.emit("vector", "tensor_copy",
                         dict(out=g["xrow2"][1][0:1, :], in_=xrow[1][0:1, :]),
                         wait=[(s_sc, vrC)], inc=s_evV)
            pB, pC = PS[0 + (n % 2)], PS[2 + (n % 2)]
            tB = P.emit("tensor", "matmul",
                        dict(out=pB[:], lhsT=g["ones16"][0:1, :],
                             rhs=g["xrow2"][0][0:1, :], start=True, stop=True),
                        wait=pp.wait(pB) + [(s_evV, rvB)], inc=s_pe)
            tC = P.emit("tensor", "matmul",
                        dict(out=pC[:], lhsT=g["ones16"][0:1, :],
                             rhs=g["xrow2"][1][0:1, :], start=True, stop=True),
                        wait=pp.wait(pC) + [(s_evV, rvC)], inc=s_pe)
            rowv[key] = rvC
            rowv_pe[key] = tC
            bvguard = g.setdefault("bv_dump_v", {})
            wBc = [(s_pe, tB)] + ([(s_sc, bvguard[(c, n - 2)])] if (c, n - 2) in bvguard else [])
            vB = P.emit("vector", "tensor_copy", dict(out=Bv[n % 2][:], in_=pB[:]),
                        wait=wBc, inc=s_evV)
            vC = P.emit("vector", "tensor_copy", dict(out=Cv[n % 2][:], in_=pC[:]),
                        wait=[(s_pe, tC)], inc=s_evV)
            pp.free(pB, s_evV, vB)
            pp.free(pC, s_evV, vC)
            if "dbg_bv" in g:
                P.dma("scalar", g["dbg_bv"][n * 128:(n + 1) * 128, cs], Bv[n % 2][:],
                      s_sc, wait=[(s_evV, vB)])
                bvguard[(c, n)] = P.dma("scalar", g["dbg_cv"][n * 128:(n + 1) * 128, cs],
                                        Cv[n % 2][:], s_sc, wait=[(s_evV, vC)])
            for dt in range(KTI):
                i = n * KTI + dt
                ib = i % 2
                wA = [(s_evV, vdt[dt])]
                if i >= 2:
                    wA.append((s_evV, scan_v[i - 2]))
                if c == 0 and i == 0:
                    wA.append((s_evV, g["A_ready"]))
                da_v[i] = P.emit("scalar", "activation",
                                 dict(out=dA[ib][:],
                                      in_=dt16[:, dt * TC2:(dt + 1) * TC2],
                                      func=AF.Exp,
                                      scale=g["A_neg"][:, dt * NST + n: dt * NST + n + 1]),
                                 wait=wA, inc=s_evA)
                wB = [(s_evV, vu[dt]), (s_evV, vB)]
                if i >= 2:
                    wB.append((s_evV, scan_v[i - 2]))
                dbx_v[i] = P.emit("vector", "tensor_tensor",
                                  dict(out=dBx[ib][:],
                                       in0=u16[:, dt * TC2:(dt + 1) * TC2],
                                       in1=Bv[n % 2][:], op=ALU.mult),
                                  wait=wB, inc=s_evV)
                if i > 0:
                    emit_tail(i - 1)
        emit_tail(NST * KTI - 1)
        if "dbg_y" in g:
            yw = [(yadd_sem, max(yadd_v.values())), (s_evV, s_evV.n)]
            P.dma("gpsimd", g["dbg_y"].rearrange("(kt p) t -> p kt t", p=128)[:, :, cs],
                  xc32[:, 0:KTI * TC2].rearrange("p (kt t) -> p kt t", kt=KTI),
                  s_st, wait=yw)
            P.dma("gpsimd", g["dbg_dt"].rearrange("(kt p) t -> p kt t", p=128)[:, :, cs],
                  dt16[:, 0:KTI * TC2].rearrange("p (kt t) -> p kt t", kt=KTI),
                  s_st, wait=[(s_evV, vdt[KTI - 1])])
        vxch = P.dma("gpsimd", xch32[:, 0:4096].rearrange("p (kt t) -> p kt t", kt=KTH),
                     src_r[:, :, cs], s_x, wait=[(s_evA, s_evA.n)])
        gat_v = {}
        sz_v = {}
        for dt in range(2):
            sz_v[dt] = P.dma("scalar", g["szst"][dt % 2][:],
                             sz[dt * 128:(dt + 1) * 128, cs], s_sc,
                             wait=[(s_evV, s_evV.n), (s_evP, s_evP.n)])
        for dt in range(KTI):
            gat_v[dt] = P.emit("vector", "tensor_tensor",
                               dict(out=u16[:, dt * TC2:(dt + 1) * TC2],
                                    in0=xc32[:, dt * TC2:(dt + 1) * TC2],
                                    in1=g["szst"][dt % 2][:], op=ALU.mult),
                               wait=[(yadd_sem, yadd_v[dt]), (s_sc, sz_v[dt])],
                               inc=s_evV)
            if dt + 2 < KTI:
                sz_v[dt + 2] = P.dma("scalar", g["szst"][(dt + 2) % 2][:],
                                     sz[(dt + 2) * 128:(dt + 3) * 128, cs], s_sc,
                                     wait=[(s_evV, gat_v[dt])])
        vgate_all = gat_v[KTI - 1]
        vev_last = None
        for oh in range(KTH):
            po = PS[4 + (oh % 2)]
            pairs = [(slot_wo[:, kt * 1024 + oh * 128: kt * 1024 + oh * 128 + 128],
                      u16[:, kt * TC2: kt * TC2 + 512]) for kt in range(KTI)]
            tO = mm_group(P, po[:], pairs,
                          wait=pp.wait(po) +
                               ([(s_w, g["wout_ready"]), (s_evV, vgate_all)]
                                if oh == 0 else []),
                          inc=s_pe)
            vev_last = P.emit("vector", "tensor_tensor",
                              dict(out=xch32[:, oh * TC2:(oh + 1) * TC2],
                                   in0=xch32[:, oh * TC2:(oh + 1) * TC2], in1=po[:],
                                   op=ALU.add),
                              wait=[(s_pe, tO), (s_x, vxch)], inc=s_evV)
            pp.free(po, s_evV, vev_last)
        P.dma("gpsimd", dst_r[:, :, cs],
              xch32[:, 0:4096].rearrange("p (kt t) -> p kt t", kt=KTH),
              s_st, wait=[(s_evV, vev_last)])
    P.wait("tensor", (s_pe, s_pe.n))
    P.emit("tensor", "nop", {}, inc=s_wfree)



"""Main assembly: build_program (all phases) + kernel(**inputs).
Appended after fw/glb/p1/p2 contents when generating kernel.py."""
H, T = 1024, 4096
NE = 8


def build_program(stage="full"):
    nc = bass.Bass("TRN2", target_bir_lowering=False)
    x_T = nc.dram_tensor("x_T", [H, T], F32, kind="ExternalInput").ap()
    out_T = nc.dram_tensor("out_T", [H, T], F32, kind="ExternalOutput").ap()
    xres1 = nc.dram_tensor("xres1", [H, T], F32, kind="Internal").ap()
    xres2 = nc.dram_tensor("xres2", [H, T], F32, kind="Internal").ap()
    f_in, wbf = declare_ffn_inputs(nc)
    from contextlib import ExitStack
    with ExitStack() as stack:
        P = Prog(nc, stack)
        g = alloc_globals(P)
        declare_p2(P, g, nc)
        emit_consts(P, g, nc)
        cast_order = ["router", "su", "sg", "sd"]
        for e in range(NE):
            cast_order += [f"up{e}", f"gate{e}", f"down{e}"]
        emit_ffn_setup(P, g, nc, f_in, wbf, cast_order)
        emit_p2_casts(P, g)
        emit_p2_prelude(P, g)
        g["wbf"] = wbf
        emit_ffn_phase(P, g, src_hbm=x_T,
                       dst_hbm=(out_T if stage == "p1" else xres1), n_experts=NE,
                       n_chunks=4, tc=1024)
        if stage != "p1":
            barrier(P, g)
            emit_p2a(P, g, src_hbm=xres1)
            barrier(P, g)
            emit_p2b(P, g, src_hbm=xres1,
                     dst_hbm=(out_T if stage == "p12" else xres2))
        if stage not in ("p1", "p12"):
            barrier(P, g)
            emit_ffn_phase(P, g, src_hbm=xres2, dst_hbm=out_T, n_experts=NE,
                           n_chunks=4, tc=1024)
        P.wait("gpsimd", (g["s_st"], g["s_st"].n))
        P.wait("sync", (g["s_st"], g["s_st"].n))
        block = stack.enter_context(nc.Block())
        P.finalize(block)
    return nc


_NC_CACHE = []


def kernel(L0_attn_x, L0_attn_in_proj_w, L0_attn_conv_w, L0_attn_conv_b,
           L0_attn_x_proj_w, L0_attn_dt_proj_w, L0_attn_dt_proj_b, L0_attn_A_log,
           L0_attn_D, L0_attn_out_proj_w, L0_ffn_shared_gate, L0_ffn_shared_up,
           L0_ffn_shared_down, L0_ffn_gate_w, L0_ffn_up_w, L0_ffn_down_w,
           L0_ffn_router_w, L0_ffn_routing_bias):
    import ml_dtypes
    if not _NC_CACHE:
        _NC_CACHE.append(build_program())
    nc = _NC_CACHE[0]
    f32 = lambda a: np.ascontiguousarray(np.asarray(a, dtype=np.float32))

    def lay(vec):
        return np.ascontiguousarray(np.asarray(vec, np.float32).reshape(16, 128).T)

    WT = f32(L0_attn_in_proj_w).T
    base = {
        "f_su": f32(np.asarray(L0_ffn_shared_up).T),
        "f_sg": f32(np.asarray(L0_ffn_shared_gate).T),
        "f_sd": f32(np.asarray(L0_ffn_shared_down).T),
        "f_router": f32(np.asarray(L0_ffn_router_w).T),
        "c_bias": f32(L0_ffn_routing_bias).reshape(1, 8),
        "c_sel8": np.stack([(np.arange(8)[:, None] == e) * np.ones((8, 128), np.float32)
                            for e in range(8)], axis=1).reshape(8, 1024
                           ).astype(ml_dtypes.bfloat16),
        "c_ident": np.eye(128, dtype=np.float32),
        "c_ones": np.ones((128, 1), np.float32),
        "c_ones_r": np.ones((1, 128), np.float32),
        "f_winx": f32(WT[:, :2048]),
        "f_winz": f32(WT[:, 2048:]),
        "f_wout": f32(np.asarray(L0_attn_out_proj_w).T),
        "f_xp": f32(np.asarray(L0_attn_x_proj_w).T),
        "c_dtw": lay(np.asarray(L0_attn_dt_proj_w)[:, 0]),
        "c_dtb": lay(L0_attn_dt_proj_b),
        "c_Dd": lay(L0_attn_D),
        "c_convb": lay(L0_attn_conv_b),
        "c_convw": np.ascontiguousarray(
            np.asarray(L0_attn_conv_w, np.float32)[:, 0, :]
            .reshape(16, 128, 4).transpose(1, 0, 2).reshape(128, 64)),
        "c_alog": np.ascontiguousarray(
            np.asarray(L0_attn_A_log, np.float32)
            .reshape(16, 128, 16).transpose(1, 0, 2).reshape(128, 256)),
    }
    for e in range(NE):
        base[f"f_up{e}"] = f32(np.asarray(L0_ffn_up_w)[e])
        base[f"f_gate{e}"] = f32(np.asarray(L0_ffn_gate_w)[e])
        base[f"f_down{e}"] = f32(np.asarray(L0_ffn_down_w)[e])
    x = np.asarray(L0_attn_x, np.float32)
    in_maps = [dict(base, x_T=np.ascontiguousarray(x[b].T)) for b in range(8)]
    res = run_bass_kernel_spmd(nc, in_maps, core_ids=list(range(8)))
    out = np.stack([np.asarray(res.results[b]["out_T"]).T for b in range(8)])
    return out.astype(np.float32)


# version 03
# speedup vs baseline: 1.0095x; 1.0095x over previous
import sys
sys.path.insert(0, "/opt/trn_rl_repo")
import numpy as np
import concourse.bass as bass
import concourse.mybir as mybir
from concourse.bass_utils import run_bass_kernel_spmd




"""Raw-bass emission framework: per-engine streams + build-time semaphore counters.

Emission is kwargs-style (all APs/args concretized at emit time -> no late-binding
bugs). Sync policy: cross-engine RAW/WAW/WAR edges get producer.then_inc /
consumer wait_ge; one sem per producing engine gives a fence property (waiting a
later value subsumes earlier events on that engine). Same-engine plain streaming
TT/copy chains are trusted (probe-verified); memset/scan-initial/scalar-operand
same-engine chains get explicit edges.
"""

F32 = mybir.dt.float32
BF16 = mybir.dt.bfloat16
AF = mybir.ActivationFunctionType
ALU = mybir.AluOpType
AX = mybir.AxisListType

ENGINES = ("sync", "scalar", "vector", "tensor", "gpsimd")


class Sem:
    __slots__ = ("h", "n", "name")

    def __init__(self, h, name):
        self.h = h
        self.n = 0
        self.name = name


class Prog:
    def __init__(self, nc, stack):
        self.nc = nc
        self.stack = stack
        self.streams = {e: [] for e in ENGINES}
        self.ninstr = 0

    def sem(self, name):
        return Sem(self.stack.enter_context(self.nc.semaphore(name)), name)

    def sbuf(self, name, shape, dt):
        return self.stack.enter_context(self.nc.sbuf_tensor(name, shape, dt))

    def psum(self, name, shape, dt=F32):
        return self.stack.enter_context(self.nc.psum_tensor(name, shape, dt))

    def emit(self, eng, method, kwargs, wait=(), inc=None, inc_by=1):
        wait = [(s, v) for (s, v) in wait if s is not None]
        tgt = None
        if inc is not None:
            inc.n += inc_by
            tgt = inc.n
        self.streams[eng].append((method, kwargs, wait, inc, inc_by))
        self.ninstr += 1 + len(wait)
        return tgt

    def dma(self, eng, out, in_, sem, wait=(), **kw):
        return self.emit(eng, "dma_start", dict(out=out, in_=in_, **kw),
                         wait=wait, inc=sem, inc_by=16)

    def wait(self, eng, *pairs):
        self.streams[eng].append((None, None, [(s, v) for s, v in pairs], None, 0))
        self.ninstr += len(pairs)

    def finalize(self, block):
        for eng in ENGINES:
            ops = self.streams[eng]
            if not ops:
                continue
            section = getattr(block, eng)

            def body(engine, _ops=ops):
                for method, kwargs, wait, inc, inc_by in _ops:
                    for s, v in wait:
                        engine.wait_ge(s.h, v)
                    if method is None:
                        continue
                    ins = getattr(engine, method)(**kwargs)
                    if inc is not None:
                        ins.then_inc(inc.h, inc_by)

            section(body)
        self.streams = {e: [] for e in ENGINES}


def mm_group(P, out, pairs, wait=(), inc=None, first_extra=()):
    n = len(pairs)
    v = None
    for i, (l, r) in enumerate(pairs):
        v = P.emit("tensor", "matmul",
                   dict(out=out, lhsT=l, rhs=r, start=(i == 0), stop=(i == n - 1)),
                   wait=(list(wait) + list(first_extra)) if i == 0 else (),
                   inc=inc if i == n - 1 else None)
    return v


def rearr_kt(ap, p=128):
    return ap.rearrange("(kt p) o -> p kt o", p=p)


def sb3(tile, kt):
    return tile[:].rearrange("p (kt o) -> p kt o", kt=kt)


class PsumPool:

    def __init__(self):
        self.freed = {}

    def wait(self, t):
        f = self.freed.get(id(t))
        return [f] if f is not None else []

    def free(self, t, sem, val):
        self.freed[id(t)] = (sem, val)



"""Global buffer/sem allocation + prelude emission (constants, casts, router)."""

H, T, DIN, NST, NE, II = 1024, 4096, 2048, 16, 8, 2048
KTH, KTI = 8, 16


def declare_ffn_inputs(nc):
    d = {}
    b = {}
    shapes = {"su": (H, II), "sg": (H, II), "sd": (II, H)}
    for e in range(NE):
        shapes[f"up{e}"] = (H, II)
        shapes[f"gate{e}"] = (H, II)
        shapes[f"down{e}"] = (II, H)
    shapes["router"] = (H, 8)
    for name, shp in shapes.items():
        d[name] = nc.dram_tensor("f_" + name, list(shp), F32, kind="ExternalInput").ap()
        b[name] = nc.dram_tensor("b_" + name, list(shp), BF16, kind="Internal").ap()
    return d, b


def alloc_globals(P):
    g = {}
    g["pp"] = PsumPool()
    for s in ("s_x", "s_w", "s_wfree", "s_pe", "s_evA", "s_evV", "s_st", "s_cast",
              "s_pre", "s_sc", "s_evP"):
        g[s] = P.sem(s)
    g["mat_counter"] = 0
    g["PS"] = [P.psum(f"PS{i}", [128, 512]) for i in range(8)]
    g["slots"] = [P.sbuf(f"slot{i}", [128, 16384], BF16) for i in range(3)]
    g["xc32"] = P.sbuf("xc32", [128, 8192], F32)
    g["xn16"] = P.sbuf("xn16", [128, 8192], BF16)
    g["p16"] = P.sbuf("p16", [128, 8192], BF16)
    g["gs32"] = [P.sbuf(f"gs32_{i}", [128, 512], F32) for i in range(2)]
    g["sq32"] = g["gs32"]
    g["t32"] = [P.sbuf(f"t32_{i}", [128, 512], F32) for i in range(2)]
    wb1 = P.sbuf("wb32_0", [128, 512], F32)
    g["wb32"] = [wb1, wb1]
    g["rows"] = [P.sbuf(f"row{i}", [1, 512], F32) for i in range(3)]
    g["w_fm8"] = P.sbuf("w_fm8", [8, 1024], BF16)
    g["sel8"] = P.sbuf("sel8", [8, 1024], BF16)
    g["lg32"] = P.sbuf("lg32", [128, 64], F32)
    g["wr16"] = P.sbuf("wr16", [128, 64], BF16)
    g["bias_b32"] = P.sbuf("bias_b32", [128, 8], F32)
    g["ident32"] = P.sbuf("ident32", [128, 128], F32)
    g["ones_col32"] = P.sbuf("ones_col32", [128, 1], F32)
    g["ones_row32"] = P.sbuf("ones_row32", [1, 128], F32)
    g["ones16"] = P.sbuf("ones16", [1, 128], BF16)
    g["bias_row"] = P.sbuf("bias_row", [1, 8], F32)
    g["eps_col"] = P.sbuf("eps_col", [128, 1], F32)
    sm = {}
    for n, w in (("m12", 2), ("msk", 8), ("msked", 8), ("e12", 2), ("d", 1),
                 ("r", 1), ("t1", 1), ("w1", 1), ("w2", 1), ("w8", 8)):
        sm[n] = P.sbuf("sm_" + n, [128, w], F32)
    g["sm"] = sm
    return g


def emit_consts(P, g, nc):
    s_pre, s_evV = g["s_pre"], g["s_evV"]
    c_id = nc.dram_tensor("c_ident", [128, 128], F32, kind="ExternalInput").ap()
    c_ones = nc.dram_tensor("c_ones", [128, 1], F32, kind="ExternalInput").ap()
    c_ones_r = nc.dram_tensor("c_ones_r", [1, 128], F32, kind="ExternalInput").ap()
    P.dma("sync", g["ident32"][:], c_id, s_pre)
    P.dma("sync", g["ones_col32"][:], c_ones, s_pre)
    P.dma("sync", g["ones_r32_ld" if False else "ones_row32"][:], c_ones_r, s_pre)
    g["pre_consts"] = s_pre.n
    P.emit("vector", "memset", dict(ap=g["eps_col"][:], constant=1e-5), inc=s_evV)
    P.emit("vector", "memset", dict(ap=g["ones16"][:], constant=1.0), inc=s_evV)
    g["ones16_ready"] = g["s_evV"].n
    g["cast_val"] = {}


def emit_ffn_setup(P, g, nc, f_in, wbf, cast_order):
    s_cast, s_pre, s_pe, s_evV = g["s_cast"], g["s_pre"], g["s_pe"], g["s_evV"]
    c_bias = nc.dram_tensor("c_bias", [1, 8], F32, kind="ExternalInput").ap()
    c_sel8 = nc.dram_tensor("c_sel8", [8, 1024], BF16, kind="ExternalInput").ap()
    P.dma("sync", g["bias_row"][:], c_bias, s_pre)
    P.dma("sync", g["sel8"][:], c_sel8, s_pre)
    pre_ld = s_pre.n
    for name in cast_order:
        v = P.dma("gpsimd", wbf[name], f_in[name], s_cast)
        g["cast_val"][name] = v
    v = P.dma("sync", sb3(g["wr16"], KTH),
              wbf["router"].rearrange("(kt p) e -> p kt e", p=128), s_pre,
              wait=[(s_cast, g["cast_val"]["router"])])
    g["wr_ready"] = v
    t = P.emit("tensor", "matmul",
               dict(out=g["PS"][0][:, 0:8], lhsT=g["ones_row32"][0:1, :],
                    rhs=g["bias_row"][0:1, :], start=True, stop=True),
               wait=[(s_pre, pre_ld), (s_pre, g["pre_consts"])], inc=s_pe)
    v = P.emit("vector", "tensor_copy", dict(out=g["bias_b32"][:], in_=g["PS"][0][:, 0:8]),
               wait=[(s_pe, t)], inc=s_evV)
    g["pp"].free(g["PS"][0], s_evV, v)
    P.wait("tensor", (g["s_pre"], g["wr_ready"]))


def barrier(P, g):
    pairs = [(g[n], g[n].n) for n in ("s_x", "s_w", "s_pe", "s_evA", "s_evV",
                                      "s_st", "s_cast", "s_pre", "s_sc", "s_evP")]
    for eng in ("sync", "scalar", "vector", "tensor", "gpsimd"):
        P.wait(eng, *pairs)



"""FFN (MoE) phase emitter. Feature-major layout.

Every streamed matrix is [K, O] bf16, K*O = 2M elems -> slot [128, 16384] bf16.
lhsT for (kt, o0) = slot[:, kt*O + o0 : kt*O + o0 + 128].
"""


def emit_ln(P, g, *, tc, xsl, xn_dst, xw, pe_fence):
    KTH = 8
    NT = tc // 512
    pp = g["pp"]
    s_pe, s_evA, s_evV = g["s_pe"], g["s_evA"], g["s_evV"]
    PS, sq32, t32, rows = g["PS"], g["sq32"], g["t32"], g["rows"]
    xn_ready = None
    for nt in range(NT):
        pairs = [(g["ones_col32"][:, 0:1], xsl(kt, nt)) for kt in range(KTH)]
        mm_group(P, PS[6][0:1, :], pairs, wait=list(xw) + pp.wait(PS[6]))
        sq_done = []
        for kt in range(KTH):
            w = list(xw) + ([(s_pe, pe_fence)] if (nt == 0 and kt == 0) else [])
            if kt >= 2:
                w.append((s_pe, sq_done[kt - 2]))
            va = P.emit("scalar", "activation",
                        dict(out=sq32[kt % 2][:], in_=xsl(kt, nt), func=AF.Square),
                        wait=w, inc=s_evA)
            t = P.emit("tensor", "matmul",
                       dict(out=PS[7][0:1, :], lhsT=g["ones_col32"][:, 0:1],
                            rhs=sq32[kt % 2][:], start=(kt == 0), stop=(kt == KTH - 1)),
                       wait=[(s_evA, va)] + (pp.wait(PS[7]) if kt == 0 else []),
                       inc=s_pe)
            sq_done.append(t)
        vm = P.emit("vector", "tensor_scalar",
               dict(out=rows[0][0:1, :], in0=PS[6][0:1, :], scalar1=1.0 / 1024,
                    scalar2=None, op0=ALU.mult),
               wait=[(s_pe, sq_done[-1])], inc=s_evV)
        vms = P.emit("vector", "tensor_scalar",
                   dict(out=rows[1][0:1, :], in0=PS[7][0:1, :], scalar1=1.0 / 1024,
                        scalar2=None, op0=ALU.mult), inc=s_evV)
        pp.free(PS[6], s_evV, vms)
        pp.free(PS[7], s_evV, vms)
        v2 = P.emit("vector", "tensor_tensor",
               dict(out=rows[2][0:1, :], in0=rows[0][0:1, :], in1=rows[0][0:1, :], op=ALU.mult),
               wait=[(s_evV, vm)], inc=s_evV)
        vvar = P.emit("vector", "tensor_tensor",
                   dict(out=rows[2][0:1, :], in0=rows[1][0:1, :], in1=rows[2][0:1, :],
                        op=ALU.subtract), wait=[(s_evV, v2)], inc=s_evV)
        vln = P.emit("scalar", "activation",
                   dict(out=rows[2][0:1, :], in_=rows[2][0:1, :], func=AF.Ln,
                        bias=g["eps_col"][0:1, 0:1]),
                   wait=[(s_evV, vvar)], inc=s_evA)
        vexp = P.emit("scalar", "activation",
                   dict(out=rows[1][0:1, :], in_=rows[2][0:1, :], func=AF.Exp, scale=-0.5),
                   wait=[(s_evA, vln)], inc=s_evA)
        va = P.emit("vector", "scalar_tensor_tensor",
                    dict(out=rows[2][0:1, :], in0=rows[0][0:1, :], scalar=-1.0,
                         in1=rows[1][0:1, :], op0=ALU.mult, op1=ALU.mult),
                    wait=[(s_evA, vexp)], inc=s_evV)
        P.emit("tensor", "matmul",
               dict(out=PS[6][:], lhsT=g["ones_row32"][0:1, :], rhs=rows[1][0:1, :],
                    start=True, stop=True),
               wait=[(s_evV, va)] + pp.wait(PS[6]))
        tb = P.emit("tensor", "matmul",
                    dict(out=PS[7][:], lhsT=g["ones_row32"][0:1, :], rhs=rows[2][0:1, :],
                         start=True, stop=True),
                    wait=pp.wait(PS[7]), inc=s_pe)
        for kt in range(KTH):
            vt = P.emit("vector", "tensor_tensor",
                   dict(out=t32[kt % 2][:], in0=xsl(kt, nt), in1=PS[6][:], op=ALU.mult),
                   wait=[(s_pe, tb)] if kt == 0 else (), inc=s_evV)
            xn_ready = P.emit("vector", "tensor_tensor",
                              dict(out=xn_dst(kt, nt), in0=t32[kt % 2][:], in1=PS[7][:],
                                   op=ALU.add),
                              wait=[(s_evV, vt)], inc=s_evV)
        pp.free(PS[6], s_evV, xn_ready)
        pp.free(PS[7], s_evV, xn_ready)
    return xn_ready


def emit_ffn_phase(P, g, *, src_hbm, dst_hbm, n_experts=8, n_chunks=4, tc=1024,
                   chunk_list=None):
    KTH, KTI = 8, 16
    NT = tc // 512
    pp: PsumPool = g["pp"]
    s_x, s_w, s_wfree, s_pe, s_evA, s_evV, s_st, s_cast = (
        g["s_x"], g["s_w"], g["s_wfree"], g["s_pe"], g["s_evA"], g["s_evV"],
        g["s_st"], g["s_cast"])
    xc32, xn16, p16 = g["xc32"], g["xn16"], g["p16"]
    slots, PS = g["slots"], g["PS"]
    sq32, gs32, t32 = g["sq32"], g["gs32"], g["t32"]
    rows, wb32 = g["rows"], g["wb32"]
    w_fm8, sel8, lg32, sm = g["w_fm8"], g["sel8"], g["lg32"], g["sm"]
    if chunk_list is None:
        chunk_list = list(range(n_chunks))

    mat_names = ["su", "sg", "sd"]
    for e in range(n_experts):
        mat_names += [f"up{e}", f"gate{e}", f"down{e}"]

    w_avail = {}

    def emit_chunk_weight_loads(c):
        for name in mat_names:
            m = g["mat_counter"]
            waits = [(s_wfree, m - 2)] if m >= 3 else []
            waits.append((s_cast, g["cast_val"][name]))
            hbm = g["wbf"][name]
            kt = hbm.shape[0] // 128
            v = P.dma("sync", sb3(slots[m % 3], kt), rearr_kt(hbm), s_w, wait=waits)
            w_avail[(c, name)] = (v, slots[m % 3])
            g["mat_counter"] += 1

    src_r = src_hbm.rearrange("(kt p) t -> p kt t", p=128)
    dst_r = dst_hbm.rearrange("(kt p) t -> p kt t", p=128)
    for c in chunk_list:
        emit_chunk_weight_loads(c)
        pe_fence = s_pe.n
        vld = P.dma("scalar",
                    xc32[:, 0:KTH * tc].rearrange("p (kt t) -> p kt t", kt=KTH),
                    src_r[:, :, c * tc:(c + 1) * tc], s_x,
                    wait=[(s_st, s_st.n), (s_evA, s_evA.n)])
        xw = [(s_x, vld)]

        def xsl(kt, nt, c=c):
            return xc32[:, kt * tc + nt * 512: kt * tc + nt * 512 + 512]

        def xnd(kt, nt, c=c):
            return xn16[:, kt * tc + nt * 512: kt * tc + nt * 512 + 512]
        xn_ready = emit_ln(P, g, tc=tc, xsl=xsl, xn_dst=xnd, xw=xw, pe_fence=pe_fence)

        for tt in range(tc // 128):
            pt = PS[4 + (tt % 2)]
            pairs = [(xn16[:, kt * tc + tt * 128: kt * tc + tt * 128 + 128],
                      g["wr16"][:, kt * 8:(kt + 1) * 8]) for kt in range(KTH)]
            tl = mm_group(P, pt[:, 0:8], pairs,
                          wait=([(s_evV, xn_ready)] if tt == 0 else []) + pp.wait(pt),
                          inc=s_pe)
            lsl = lg32[:, tt * 8:(tt + 1) * 8]
            v = P.emit("vector", "tensor_tensor",
                       dict(out=lsl, in0=pt[:, 0:8], in1=g["bias_b32"][:, 0:8], op=ALU.add),
                       wait=[(s_pe, tl)], inc=s_evV)
            pp.free(pt, s_evV, v)
            m12, msk, w8 = sm["m12"], sm["msk"], sm["w8"]
            vlsl = v
            v = P.emit("vector", "tensor_reduce",
                       dict(out=m12[:, 0:1], in_=lsl, axis=AX.X, op=ALU.max),
                       wait=[(s_evV, vlsl)], inc=s_evV)
            vmsk1 = P.emit("vector", "tensor_scalar",
                   dict(out=msk[:], in0=lsl, scalar1=m12[:, 0:1], scalar2=None,
                        op0=ALU.is_equal), wait=[(s_evV, v)], inc=s_evV)
            vq = P.emit("vector", "scalar_tensor_tensor",
                   dict(out=sm["msked"][:], in0=msk[:], scalar=-1e30, in1=lsl,
                        op0=ALU.mult, op1=ALU.add), wait=[(s_evV, vmsk1)], inc=s_evV)
            v = P.emit("vector", "tensor_reduce",
                       dict(out=m12[:, 1:2], in_=sm["msked"][:], axis=AX.X, op=ALU.max),
                       wait=[(s_evV, vq)], inc=s_evV)
            v = P.emit("scalar", "activation",
                       dict(out=sm["e12"][:], in_=m12[:], func=AF.Exp, scale=-1.0),
                       wait=[(s_evV, v)], inc=s_evA)
            v = P.emit("vector", "tensor_tensor",
                   dict(out=sm["d"][:], in0=sm["e12"][:, 0:1], in1=sm["e12"][:, 1:2],
                        op=ALU.add), wait=[(s_evA, v)], inc=s_evV)
            v = P.emit("vector", "tensor_scalar",
                       dict(out=sm["d"][:], in0=sm["d"][:], scalar1=2.0, scalar2=None,
                            op0=ALU.add), wait=[(s_evV, v)], inc=s_evV)
            v = P.emit("vector", "reciprocal", dict(out=sm["r"][:], in_=sm["d"][:]),
                   wait=[(s_evV, v)], inc=s_evV)
            v = P.emit("vector", "tensor_scalar",
                       dict(out=sm["r"][:], in0=sm["r"][:], scalar1=0.5, scalar2=None,
                            op0=ALU.mult), wait=[(s_evV, v)], inc=s_evV)
            vt1 = P.emit("vector", "tensor_scalar",
                   dict(out=sm["t1"][:], in0=sm["e12"][:, 1:2], scalar1=1.0, scalar2=None,
                        op0=ALU.add), wait=[(s_evV, v)], inc=s_evV)
            vw1 = P.emit("vector", "tensor_tensor",
                   dict(out=sm["w1"][:], in0=sm["t1"][:], in1=sm["r"][:], op=ALU.mult),
                   wait=[(s_evV, vt1)], inc=s_evV)
            vt1 = P.emit("vector", "tensor_scalar",
                   dict(out=sm["t1"][:], in0=sm["e12"][:, 0:1], scalar1=1.0, scalar2=None,
                        op0=ALU.add), wait=[(s_evV, vw1)], inc=s_evV)
            v = P.emit("vector", "tensor_tensor",
                       dict(out=sm["w2"][:], in0=sm["t1"][:], in1=sm["r"][:], op=ALU.mult),
                       wait=[(s_evV, vt1)], inc=s_evV)
            vw8 = P.emit("vector", "tensor_scalar",
                   dict(out=w8[:], in0=msk[:], scalar1=sm["w1"][:, 0:1], scalar2=None,
                        op0=ALU.mult), wait=[(s_evV, v), (s_evV, vmsk1)], inc=s_evV)
            vmsk2 = P.emit("vector", "tensor_scalar",
                   dict(out=msk[:], in0=lsl, scalar1=m12[:, 1:2], scalar2=None,
                        op0=ALU.is_equal), wait=[(s_evV, vw8)], inc=s_evV)
            v = P.emit("vector", "scalar_tensor_tensor",
                       dict(out=w8[:], in0=msk[:], scalar=sm["w2"][:, 0:1], in1=w8[:],
                            op0=ALU.mult, op1=ALU.add), wait=[(s_evV, vmsk2)], inc=s_evV)
            ptp = PS[4 + ((tt + 1) % 2)]
            t2 = P.emit("tensor", "transpose",
                        dict(out=ptp[0:8, 0:128], in_=w8[:], identity=g["ident32"][:]),
                        wait=[(s_evV, v)] + pp.wait(ptp), inc=s_pe)
            v = P.emit("vector", "tensor_copy",
                       dict(out=w_fm8[:, tt * 128:(tt + 1) * 128], in_=ptp[0:8, 0:128]),
                       wait=[(s_pe, t2)], inc=s_evV)
            pp.free(ptp, s_evV, v)
        v_wr = v

        gs_last = {0: None, 1: None}
        tD_last = None
        for ei in range(-1, n_experts):
            names = ("su", "sg", "sd") if ei < 0 else (f"up{ei}", f"gate{ei}", f"down{ei}")
            (vU, slot_u), (vG, slot_g), (vD, slot_d) = (w_avail[(c, n)] for n in names)
            for nt in range(NT):
                if ei >= 0:
                    pw = PS[6 + (nt % 2)]
                    twb = P.emit("tensor", "matmul",
                                 dict(out=pw[:], lhsT=sel8[:, ei * 128:(ei + 1) * 128],
                                      rhs=w_fm8[0:8, nt * 512: nt * 512 + 512],
                                      start=True, stop=True),
                                 wait=pp.wait(pw) + [(s_evV, v_wr)], inc=s_pe)
                    vwb = P.emit("vector", "tensor_copy",
                                 dict(out=wb32[nt % 2][:], in_=pw[:]),
                                 wait=[(s_pe, twb)], inc=s_evV)
                    pp.free(pw, s_evV, vwb)
                    my_wb = wb32[nt % 2]
                else:
                    my_wb = None
                vp = None
                for ot in range(KTI):
                    pu, pg = PS[0 + (ot % 2)], PS[2 + (ot % 2)]
                    w0 = pp.wait(pu) + ([(s_w, vU)] if (nt == 0 and ot == 0) else [])
                    pairs_u = [(slot_u[:, kt * 2048 + ot * 128: kt * 2048 + ot * 128 + 128],
                                xn16[:, kt * tc + nt * 512: kt * tc + nt * 512 + 512])
                               for kt in range(KTH)]
                    tU = mm_group(P, pu[:], pairs_u, wait=w0, inc=s_pe)
                    pairs_g = [(slot_g[:, kt * 2048 + ot * 128: kt * 2048 + ot * 128 + 128],
                                xn16[:, kt * tc + nt * 512: kt * tc + nt * 512 + 512])
                               for kt in range(KTH)]
                    tG = mm_group(P, pg[:], pairs_g,
                                  wait=pp.wait(pg) + ([(s_w, vG)] if (nt == 0 and ot == 0) else []),
                                  inc=s_pe)
                    wA = [(s_pe, tG)]
                    if gs_last[ot % 2] is not None:
                        wA.append((s_evV, gs_last[ot % 2]))
                    vg = P.emit("scalar", "activation",
                                dict(out=gs32[ot % 2][:], in_=pg[:], func=AF.Silu),
                                wait=wA, inc=s_evA)
                    pp.free(pg, s_evA, vg)
                    vp = P.emit("vector", "tensor_tensor",
                                dict(out=p16[:, ot * 512:(ot + 1) * 512],
                                     in0=gs32[ot % 2][:], in1=pu[:], op=ALU.mult),
                                wait=[(s_evA, vg), (s_pe, tU)] +
                                     ([(s_pe, tD_last)] if (ot == 0 and tD_last is not None) else []),
                                inc=s_evV)
                    pp.free(pu, s_evV, vp)
                    gs_last[ot % 2] = vp
                if nt == NT - 1:
                    P.wait("tensor", (s_pe, s_pe.n))
                    P.emit("tensor", "nop", {}, inc=s_wfree)
                    P.emit("tensor", "nop", {}, inc=s_wfree)
                for oh in range(KTH):
                    pd = PS[4 + (oh % 2)]
                    pairs_d = [(slot_d[:, kt * 1024 + oh * 128: kt * 1024 + oh * 128 + 128],
                                p16[:, kt * 512:(kt + 1) * 512]) for kt in range(KTI)]
                    tD = mm_group(P, pd[:], pairs_d,
                                  wait=pp.wait(pd) + [(s_evV, vp)] +
                                       ([(s_w, vD)] if (nt == 0 and oh == 0) else []),
                                  inc=s_pe)
                    if my_wb is None:
                        vd = P.emit("vector", "tensor_scalar",
                                    dict(out=t32[oh % 2][:], in0=pd[:], scalar1=0.5,
                                         scalar2=None, op0=ALU.mult),
                                    wait=[(s_pe, tD)], inc=s_evV)
                    else:
                        vd = P.emit("vector", "tensor_tensor",
                                    dict(out=t32[oh % 2][:], in0=pd[:], in1=my_wb[:],
                                         op=ALU.mult),
                                    wait=[(s_pe, tD)], inc=s_evV)
                    pp.free(pd, s_evV, vd)
                    vcomb = P.emit("vector", "tensor_tensor",
                           dict(out=xc32[:, oh * tc + nt * 512: oh * tc + nt * 512 + 512],
                                in0=xc32[:, oh * tc + nt * 512: oh * tc + nt * 512 + 512],
                                in1=t32[oh % 2][:], op=ALU.add),
                           wait=[(s_evV, vd)], inc=s_evV)
                    tD_last = tD
            P.wait("tensor", (s_pe, s_pe.n))
            P.emit("tensor", "nop", {}, inc=s_wfree)

        P.dma("scalar", dst_r[:, :, c * tc:(c + 1) * tc],
              xc32[:, 0:KTH * tc].rearrange("p (kt t) -> p kt t", kt=KTH),
              s_st, wait=[(s_evV, vcomb)])



"""Mamba phase emitters (P2a: in_proj/conv/silu/z/x_proj; P2b: dt/scan/y/out_proj).

Feature-major. Per core: T=4096 tokens, D_INNER=2048 (16 dtiles), D_STATE=16.
Chunk = 512 tokens (8 chunks).

Host layouts:
  f_winx/f_winz [1024, 2048] f32 = in_proj_w.T halves; f_wout [2048, 1024] = out_proj_w.T
  f_xp [2048, 33] = x_proj_w.T
  c_dtw/c_dtb/c_Dd/c_convb [128, 16] f32: (p, j) = vec[j*128 + p]
  c_convw [128, 64]: (p, 4*j+k) = conv_w[j*128+p, 0, k]
  c_alog [128, 256]: (p, j*16+n) = A_log[j*128+p, n]
  c_selxp [33, 33*128] bf16: block j = [33,128] with row j all-ones (selection lhsT)

Weight slots: winx/winz/wout occupy ring slots 0/1/2 as ring matrices (mat_counter
advanced by 3; s_wfree incremented at their last use so the FFN phase 2 ring
streaming stays consistent).
"""

KTH, KTI, NST = 8, 16, 16
TC2 = 512
NCH2 = 8


def declare_p2(P, g, nc):
    g["xssm"] = [P.sbuf(f"xssm{i}", [128, 515], F32) for i in range(2)]
    g["carry32"] = P.sbuf("carry32", [128, 48], F32)
    g["xps16"] = P.sbuf("xps16", [33, 512], BF16)
    g["xrow"] = [P.sbuf(f"xrow{i}", [1, 512], BF16) for i in range(2)]
    g["xrow2"] = [P.sbuf(f"xrow2_{i}", [1, 512], BF16) for i in range(2)]
    g["A_neg"] = P.sbuf("A_neg", [128, 256], F32)
    g["dtw32"] = P.sbuf("dtw32", [128, 16], F32)
    g["dtb32"] = P.sbuf("dtb32", [128, 16], F32)
    g["Dd32"] = P.sbuf("Dd32", [128, 16], F32)
    g["convw32"] = P.sbuf("convw32", [128, 64], F32)
    g["convb32"] = P.sbuf("convb32", [128, 16], F32)
    g["wxp16"] = P.sbuf("wxp16", [128, 16 * 33], BF16)
    g["hstate"] = P.sbuf("hstate", [128, 256], BF16)
    g["dAb"] = [P.sbuf(f"dAb{i}", [128, 512], BF16) for i in range(2)]
    g["dBxb"] = [P.sbuf(f"dBxb{i}", [128, 512], BF16) for i in range(2)]
    g["hb"] = [P.sbuf(f"hb{i}", [128, 512], BF16) for i in range(2)]
    g["Bv16"] = [P.sbuf(f"Bv16_{i}", [128, 512], BF16) for i in range(2)]
    g["Cv16"] = [P.sbuf(f"Cv16_{i}", [128, 512], BF16) for i in range(2)]
    g["szst"] = [P.sbuf(f"szst{i}", [128, 512], BF16) for i in range(2)]
    g["yst"] = g["szst"]
    d, b = {}, {}
    for name, shp in (("winx", (1024, 2048)), ("winz", (1024, 2048)),
                      ("wout", (2048, 1024)), ("xp", (2048, 33))):
        d[name] = nc.dram_tensor("f_" + name, list(shp), F32, kind="ExternalInput").ap()
        b[name] = nc.dram_tensor("b_" + name, list(shp), BF16, kind="Internal").ap()
    for name, shp in (("dtw", (128, 16)), ("dtb", (128, 16)), ("Dd", (128, 16)),
                      ("convw", (128, 64)), ("convb", (128, 16)), ("alog", (128, 256))):
        d[name] = nc.dram_tensor("c_" + name, list(shp), F32, kind="ExternalInput").ap()
    g["m_in"], g["m_bf"] = d, b
    g["xc_hbm"] = nc.dram_tensor("xc_hbm", [2048, 4096], F32, kind="Internal").ap()
    g["sz_hbm"] = nc.dram_tensor("sz_hbm", [2048, 4096], BF16, kind="Internal").ap()
    g["xp_hbm"] = nc.dram_tensor("xp_hbm", [33, 4096], BF16, kind="Internal").ap()


def emit_p2_casts(P, g):
    d, b = g["m_in"], g["m_bf"]
    for name in ("winx", "winz", "wout", "xp"):
        g["cast_val"][name] = P.dma("gpsimd", b[name], d[name], g["s_cast"])


def emit_p2_prelude(P, g):
    d = g["m_in"]
    s_pre, s_evA, s_evV, s_cast = g["s_pre"], g["s_evA"], g["s_evV"], g["s_cast"]
    for name, dst in (("dtw", "dtw32"), ("dtb", "dtb32"), ("Dd", "Dd32"),
                      ("convw", "convw32"), ("convb", "convb32")):
        P.dma("sync", g[dst][:], d[name], s_pre)
    v = P.dma("sync", g["A_neg"][:], d["alog"], s_pre)
    va = P.emit("scalar", "activation",
                dict(out=g["A_neg"][:], in_=g["A_neg"][:], func=AF.Exp),
                wait=[(s_pre, v)], inc=s_evA)
    g["A_ready"] = P.emit("vector", "tensor_scalar",
                          dict(out=g["A_neg"][:], in0=g["A_neg"][:], scalar1=-1.0,
                               scalar2=None, op0=ALU.mult),
                          wait=[(s_evA, va)], inc=s_evV)
    g["wxp_ready"] = P.dma("sync", sb3(g["wxp16"], KTI),
                           g["m_bf"]["xp"].rearrange("(kt p) o -> p kt o", p=128),
                           s_pre, wait=[(s_cast, g["cast_val"]["xp"])])
    g["carry_ready"] = P.emit("vector", "memset",
                              dict(ap=g["carry32"][:], constant=0.0), inc=s_evV)
    g["h_ready"] = P.emit("vector", "memset",
                          dict(ap=g["hstate"][:], constant=0.0), inc=s_evV)


def emit_p2a(P, g, *, src_hbm):
    pp = g["pp"]
    s_x, s_w, s_wfree, s_pe, s_evA, s_evV, s_sc, s_cast, s_pre = (
        g["s_x"], g["s_w"], g["s_wfree"], g["s_pe"], g["s_evA"], g["s_evV"],
        g["s_sc"], g["s_cast"], g["s_pre"])
    PS, slots = g["PS"], g["slots"]
    xch32 = g["p16"][:].bitcast(F32)
    xn16, xc32 = g["xn16"], g["xc32"]
    carry = g["carry32"]
    t32 = g["t32"]
    zs16 = [t[:].bitcast(BF16) for t in g["wb32"]]
    xc16 = [t[:].bitcast(BF16) for t in g["gs32"]]
    src_r = src_hbm.rearrange("(kt p) t -> p kt t", p=128)
    xc_r = g["xc_hbm"].rearrange("(kt p) t -> p kt t", p=128)
    sz = g["sz_hbm"]

    m0 = g["mat_counter"]
    g["mat_counter"] += 3
    vwx = P.dma("sync", sb3(slots[m0 % 3], KTH), rearr_kt(g["m_bf"]["winx"]), s_w,
                wait=([(s_wfree, m0 - 2)] if m0 >= 3 else []) +
                     [(s_cast, g["cast_val"]["winx"])])
    vwz = P.dma("sync", sb3(slots[(m0 + 1) % 3], KTH), rearr_kt(g["m_bf"]["winz"]), s_w,
                wait=([(s_wfree, m0 - 1)] if m0 + 1 >= 3 else []) +
                     [(s_cast, g["cast_val"]["winz"])])
    vwo = P.dma("sync", sb3(slots[(m0 + 2) % 3], KTI), rearr_kt(g["m_bf"]["wout"]), s_w,
                wait=([(s_wfree, m0)] if m0 + 2 >= 3 else []) +
                     [(s_cast, g["cast_val"]["wout"])])
    g["wout_ready"] = vwo
    slot_wx, slot_wz, g["slot_wo"] = slots[m0 % 3], slots[(m0 + 1) % 3], slots[(m0 + 2) % 3]

    xssm_rd = {0: None, 1: None}
    silu_v = {0: None, 1: None}
    xc16_v = {0: None, 1: None}
    zdma_v = {0: None, 1: None}
    g["xc_store"] = {}
    for c in range(NCH2):
        pe_fence = s_pe.n
        vx = P.dma("scalar", xch32[:, 0:4096].rearrange("p (kt t) -> p kt t", kt=KTH),
                   src_r[:, :, c * TC2:(c + 1) * TC2], s_x,
                   wait=[(s_evA, s_evA.n), (s_pe, s_pe.n), (s_evV, s_evV.n)])
        xw = [(s_x, vx)]

        def xsl(kt, nt, c=c):
            return xch32[:, kt * TC2: kt * TC2 + 512]

        def xnd(kt, nt, c=c):
            return xn16[:, kt * TC2: kt * TC2 + 512]
        xn_ready = emit_ln(P, g, tc=TC2, xsl=xsl, xn_dst=xnd, xw=xw, pe_fence=pe_fence)

        xp_stop = None
        last_silu = None
        for i in range(KTI):
            px = PS[0 + (i % 2)]
            pairs = [(slot_wx[:, kt * 2048 + i * 128: kt * 2048 + i * 128 + 128],
                      xn16[:, kt * TC2: kt * TC2 + 512]) for kt in range(KTH)]
            tX = mm_group(P, px[:], pairs,
                          wait=pp.wait(px) +
                               ([(s_w, vwx), (s_evV, xn_ready)] if i == 0 else []),
                          inc=s_pe)
            xt = g["xssm"][i % 2]
            wA = [(s_pe, tX)]
            if xssm_rd[i % 2] is not None:
                wA.append((s_evV, xssm_rd[i % 2]))
            vev = P.emit("scalar", "copy", dict(out=xt[:, 3:515], in_=px[:]),
                         wait=wA, inc=s_evA)
            pp.free(px, s_evA, vev)
            vcr = P.emit("vector", "tensor_copy",
                         dict(out=xt[:, 0:3], in_=carry[:, i * 3:i * 3 + 3]),
                         wait=([(s_evV, g["carry_ready"])] if (c == 0 and i < 3) else []) +
                              [(s_evA, vev)],
                         inc=s_evV)
            wT = [(s_evV, vcr)]
            if silu_v[i % 2] is not None:
                wT.append((s_evA, silu_v[i % 2]))
            vt = P.emit("vector", "tensor_scalar",
                        dict(out=t32[i % 2][:], in0=xt[:, 0:512],
                             scalar1=g["convw32"][:, 4 * i:4 * i + 1], scalar2=None,
                             op0=ALU.mult),
                        wait=wT, inc=s_evV)
            for k in range(1, 4):
                vt = P.emit("vector", "scalar_tensor_tensor",
                            dict(out=t32[i % 2][:], in0=xt[:, k:k + 512],
                                 scalar=g["convw32"][:, 4 * i + k:4 * i + k + 1],
                                 in1=t32[i % 2][:], op0=ALU.mult, op1=ALU.add),
                            wait=[(s_evV, vt)], inc=s_evV)
            vco = P.emit("vector", "tensor_copy",
                         dict(out=carry[:, i * 3:i * 3 + 3], in_=xt[:, 512:515]),
                         inc=s_evV)
            xssm_rd[i % 2] = vco
            wS = [(s_evV, vt)]
            if c > 0 and i == 0:
                wS.append((s_sc, g["xc_store"][c - 1]))
            last_silu = P.emit("scalar", "activation",
                               dict(out=xc32[:, i * TC2:(i + 1) * TC2], in_=t32[i % 2][:],
                                    func=AF.Silu, bias=g["convb32"][:, i:i + 1]),
                               wait=wS, inc=s_evA)
            silu_v[i % 2] = last_silu
            vxc = P.emit("vector", "tensor_copy",
                         dict(out=xc16[i % 2][:, 0:512],
                              in_=xc32[:, i * TC2:(i + 1) * TC2]),
                         wait=[(s_evA, last_silu)] +
                              ([(s_evV, xc16_v[i % 2])] if xc16_v[i % 2] else []),
                         inc=s_evV)
            xp_stop = P.emit("tensor", "matmul",
                             dict(out=PS[4][0:33, 0:512],
                                  lhsT=g["wxp16"][:, i * 33:(i + 1) * 33],
                                  rhs=xc16[i % 2][:, 0:512],
                                  start=(i == 0), stop=(i == KTI - 1),
                                  skip_group_check=True),
                             wait=[(s_evV, vxc)] +
                                  ([(s_pre, g["wxp_ready"])] if i == 0 else []) +
                                  (pp.wait(PS[4]) if i == 0 else []),
                             inc=s_pe)
            xc16_v[i % 2] = xp_stop
            pz = PS[2 + (i % 2)]
            pairs = [(slot_wz[:, kt * 2048 + i * 128: kt * 2048 + i * 128 + 128],
                      xn16[:, kt * TC2: kt * TC2 + 512]) for kt in range(KTH)]
            tZ = mm_group(P, pz[:], pairs,
                          wait=pp.wait(pz) + ([(s_w, vwz)] if i == 0 else []),
                          inc=s_pe)
            wZ = [(s_pe, tZ)]
            if zdma_v[i % 2] is not None:
                wZ.append((s_sc, zdma_v[i % 2]))
            vz = P.emit("scalar", "activation",
                        dict(out=zs16[i % 2][:, 0:512], in_=pz[:], func=AF.Silu),
                        wait=wZ, inc=s_evA)
            pp.free(pz, s_evA, vz)
            zdma_v[i % 2] = P.dma("scalar",
                                  sz[i * 128:(i + 1) * 128, c * TC2:(c + 1) * TC2],
                                  zs16[i % 2][:, 0:512], s_sc, wait=[(s_evA, vz)])
        wxpv = [(s_pe, xp_stop)]
        if c > 0:
            wxpv.append((s_sc, g["xp_store"][c - 1]))
        vxp = P.emit("vector", "tensor_copy",
                     dict(out=g["xps16"][0:33, 0:512], in_=PS[4][0:33, 0:512]),
                     wait=wxpv, inc=s_evV)
        pp.free(PS[4], s_evV, vxp)
        g.setdefault("xp_store", {})[c] = P.dma(
            "scalar", g["xp_hbm"][:, c * TC2:(c + 1) * TC2], g["xps16"][0:33, 0:512],
            s_sc, wait=[(s_evV, vxp)])
        g["xc_store"][c] = P.dma(
            "scalar", xc_r[:, :, c * TC2:(c + 1) * TC2],
            xc32[:, 0:KTI * TC2].rearrange("p (kt t) -> p kt t", kt=KTI),
            s_sc, wait=[(s_evA, last_silu)])
    P.wait("tensor", (s_pe, s_pe.n))
    P.emit("tensor", "nop", {}, inc=s_wfree)
    P.emit("tensor", "nop", {}, inc=s_wfree)


def emit_p2b(P, g, *, src_hbm, dst_hbm):
    pp = g["pp"]
    s_x, s_w, s_wfree, s_pe, s_evA, s_evV, s_evP, s_sc, s_st = (
        g["s_x"], g["s_w"], g["s_wfree"], g["s_pe"], g["s_evA"], g["s_evV"],
        g["s_evP"], g["s_sc"], g["s_st"])
    PS = g["PS"]
    slot_wo = g["slot_wo"]
    dt16 = g["p16"]
    xch32 = g["p16"][:].bitcast(F32)
    u16 = g["xn16"]
    xc32 = g["xc32"]
    t32 = g["t32"]
    xrow = g["xrow"]
    dA, dBx, hb = g["dAb"], g["dBxb"], g["hb"]
    Bv, Cv = g["Bv16"], g["Cv16"]
    hstate = g["hstate"]
    src_r = src_hbm.rearrange("(kt p) t -> p kt t", p=128)
    dst_r = dst_hbm.rearrange("(kt p) t -> p kt t", p=128)
    xc_r = g["xc_hbm"].rearrange("(kt p) t -> p kt t", p=128)
    sz = g["sz_hbm"]

    rowv_g = {}
    rowv_pe = {}
    for c in range(NCH2):
        cs = slice(c * TC2, (c + 1) * TC2)
        vxc = P.dma("scalar",
                    xc32[:, 0:KTI * TC2].rearrange("p (kt t) -> p kt t", kt=KTI),
                    xc_r[:, :, cs], s_sc,
                    wait=[(s_sc, g["xc_store"][c]), (s_evV, s_evV.n),
                          (s_evP, s_evP.n), (s_evA, s_evA.n)])
        vdtr = P.dma("scalar", g["xps16"][0:1, 0:512], g["xp_hbm"][0:1, cs], s_sc,
                     wait=[(s_sc, g["xp_store"][c]), (s_pe, s_pe.n),
                           (g["s_evV"], g["s_evV"].n)])
        rdtr = P.emit("vector", "tensor_copy",
                      dict(out=g["xrow2"][0][0:1, :], in_=g["xps16"][0:1, 0:512]),
                      wait=[(s_sc, vdtr)] +
                           ([(s_pe, rowv_pe[c * NST - 1])] if (c * NST - 1) in rowv_pe
                            else []),
                      inc=s_evV)
        tdtr = P.emit("tensor", "matmul",
                      dict(out=PS[5][:], lhsT=g["ones16"][0:1, :],
                           rhs=g["xrow2"][0][0:1, :], start=True, stop=True),
                      wait=pp.wait(PS[5]) + [(s_evV, rdtr),
                                             (g["s_evV"], g["ones16_ready"])],
                      inc=s_pe)
        rowv_pe[c * NST - 1] = tdtr
        vdt = {}
        prevdt = None
        for i in range(KTI):
            v0 = P.emit("vector", "tensor_scalar",
                        dict(out=t32[0][:], in0=PS[5][:],
                             scalar1=g["dtw32"][:, i:i + 1],
                             scalar2=g["dtb32"][:, i:i + 1],
                             op0=ALU.mult, op1=ALU.add),
                        wait=[(s_pe, tdtr)] +
                             ([(s_evV, prevdt)] if prevdt is not None else []),
                        inc=s_evV)
            v1 = P.emit("scalar", "activation",
                        dict(out=t32[1][:], in_=t32[0][:], func=AF.Abs),
                        wait=[(s_evV, v0)], inc=s_evA)
            v2 = P.emit("scalar", "activation",
                        dict(out=t32[1][:], in_=t32[1][:], func=AF.Exp, scale=-1.0),
                        wait=[(s_evA, v1)], inc=s_evA)
            v3 = P.emit("scalar", "activation",
                        dict(out=t32[1][:], in_=t32[1][:], func=AF.Ln, bias=1.0),
                        wait=[(s_evA, v2)], inc=s_evA)
            v4 = P.emit("vector", "tensor_scalar",
                        dict(out=t32[0][:], in0=t32[0][:], scalar1=0.0, scalar2=None,
                             op0=ALU.max),
                        wait=[(s_evA, v1)], inc=s_evV)
            v5 = P.emit("vector", "tensor_tensor",
                        dict(out=dt16[:, i * TC2:(i + 1) * TC2], in0=t32[0][:],
                             in1=t32[1][:], op=ALU.add),
                        wait=[(s_evV, v4), (s_evA, v3)] +
                             ([(s_st, s_st.n)] if i == 0 else []), inc=s_evV)
            vdt[i] = v5
            prevdt = v5
        pp.free(PS[5], s_evV, vdt[KTI - 1])
        vu, vyb = {}, {}
        for i in range(KTI):
            vu[i] = P.emit("vector", "tensor_tensor",
                           dict(out=u16[:, i * TC2:(i + 1) * TC2],
                                in0=dt16[:, i * TC2:(i + 1) * TC2],
                                in1=xc32[:, i * TC2:(i + 1) * TC2], op=ALU.mult),
                           wait=[(s_evV, vdt[i]), (s_sc, vxc)], inc=s_evV)
        for i in range(KTI):
            vyb[i] = P.emit("vector", "tensor_scalar",
                            dict(out=xc32[:, i * TC2:(i + 1) * TC2],
                                 in0=xc32[:, i * TC2:(i + 1) * TC2],
                                 scalar1=g["Dd32"][:, i:i + 1], scalar2=None,
                                 op0=ALU.mult),
                            wait=[(s_evV, vu[i])], inc=s_evV)
        scan_v, ymul_v, hst_v, dbx_v, da_v, yadd_v = {}, {}, {}, {}, {}, {}
        yadd_sem = s_evV if g.get("yadd_dve") else s_evP
        ymul_guard = {0: (yadd_sem, yadd_sem.n), 1: (yadd_sem, yadd_sem.n)}

        def emit_tail(i):
            n_, dt_ = divmod(i, KTI)
            ib = i % 2
            col = dt_ * NST + n_
            sv = P.emit("vector", "tensor_tensor_scan",
                        dict(out=hb[ib][:], data0=dA[ib][:], data1=dBx[ib][:],
                             initial=hstate[:, col:col + 1],
                             op0=ALU.mult, op1=ALU.add),
                        wait=[(s_evA, da_v[i]), (s_evV, dbx_v[i])] +
                             ([(s_evV, hst_v[i - 2])] if i >= 2 else
                              ([(s_evV, g["h_ready"])] if c == 0 else [])),
                        inc=s_evV)
            scan_v[i] = sv
            mv = P.emit("vector", "tensor_tensor",
                        dict(out=g["yst"][i % 2][:], in0=hb[ib][:], in1=Cv[n_ % 2][:],
                             op=ALU.mult),
                        wait=[(s_evV, sv), ymul_guard[i % 2]],
                        inc=s_evV)
            ymul_v[i] = mv
            av = P.emit("vector" if g.get("yadd_dve") else "gpsimd", "tensor_tensor",
                        dict(out=xc32[:, dt_ * TC2:(dt_ + 1) * TC2],
                             in0=xc32[:, dt_ * TC2:(dt_ + 1) * TC2],
                             in1=g["yst"][i % 2][:], op=ALU.add),
                        wait=[(s_evV, mv), (s_evV, vyb[dt_])],
                        inc=yadd_sem)
            yadd_v[dt_] = av
            ymul_guard[i % 2] = (yadd_sem, av)
            hst_v[i] = P.emit("vector", "tensor_copy",
                              dict(out=hstate[:, col:col + 1], in_=hb[ib][:, 511:512]),
                              wait=[(s_evV, sv)], inc=s_evV)

        rowv = rowv_g
        for n in range(NST):
            key = c * NST + n
            rg = [(s_evV, rowv[key - 1])] if (key - 1) in rowv else []
            vrB = P.dma("scalar", xrow[0][0:1, :],
                        g["xp_hbm"][1 + n:2 + n, cs], s_sc,
                        wait=rg + ([(s_sc, g["xp_store"][c])] if n == 0 else []))
            vrC = P.dma("scalar", xrow[1][0:1, :],
                        g["xp_hbm"][17 + n:18 + n, cs], s_sc)
            w2 = [(s_pe, rowv_pe[key - 1])] if (key - 1) in rowv_pe else []
            rvB = P.emit("vector", "tensor_copy",
                         dict(out=g["xrow2"][0][0:1, :], in_=xrow[0][0:1, :]),
                         wait=[(s_sc, vrB)] + w2, inc=s_evV)
            rvC = P.emit("vector", "tensor_copy",
                         dict(out=g["xrow2"][1][0:1, :], in_=xrow[1][0:1, :]),
                         wait=[(s_sc, vrC)], inc=s_evV)
            pB, pC = PS[0 + (n % 2)], PS[2 + (n % 2)]
            tB = P.emit("tensor", "matmul",
                        dict(out=pB[:], lhsT=g["ones16"][0:1, :],
                             rhs=g["xrow2"][0][0:1, :], start=True, stop=True),
                        wait=pp.wait(pB) + [(s_evV, rvB)], inc=s_pe)
            tC = P.emit("tensor", "matmul",
                        dict(out=pC[:], lhsT=g["ones16"][0:1, :],
                             rhs=g["xrow2"][1][0:1, :], start=True, stop=True),
                        wait=pp.wait(pC) + [(s_evV, rvC)], inc=s_pe)
            rowv[key] = rvC
            rowv_pe[key] = tC
            bvguard = g.setdefault("bv_dump_v", {})
            wBc = [(s_pe, tB)] + ([(s_sc, bvguard[(c, n - 2)])] if (c, n - 2) in bvguard else [])
            vB = P.emit("vector", "tensor_copy", dict(out=Bv[n % 2][:], in_=pB[:]),
                        wait=wBc, inc=s_evV)
            vC = P.emit("vector", "tensor_copy", dict(out=Cv[n % 2][:], in_=pC[:]),
                        wait=[(s_pe, tC)], inc=s_evV)
            pp.free(pB, s_evV, vB)
            pp.free(pC, s_evV, vC)
            if "dbg_bv" in g:
                P.dma("scalar", g["dbg_bv"][n * 128:(n + 1) * 128, cs], Bv[n % 2][:],
                      s_sc, wait=[(s_evV, vB)])
                bvguard[(c, n)] = P.dma("scalar", g["dbg_cv"][n * 128:(n + 1) * 128, cs],
                                        Cv[n % 2][:], s_sc, wait=[(s_evV, vC)])
            for dt in range(KTI):
                i = n * KTI + dt
                ib = i % 2
                wA = [(s_evV, vdt[dt])]
                if i >= 2:
                    wA.append((s_evV, scan_v[i - 2]))
                if c == 0 and i == 0:
                    wA.append((s_evV, g["A_ready"]))
                da_v[i] = P.emit("scalar", "activation",
                                 dict(out=dA[ib][:],
                                      in_=dt16[:, dt * TC2:(dt + 1) * TC2],
                                      func=AF.Exp,
                                      scale=g["A_neg"][:, dt * NST + n: dt * NST + n + 1]),
                                 wait=wA, inc=s_evA)
                wB = [(s_evV, vu[dt]), (s_evV, vB)]
                if i >= 2:
                    wB.append((s_evV, scan_v[i - 2]))
                dbx_v[i] = P.emit("vector", "tensor_tensor",
                                  dict(out=dBx[ib][:],
                                       in0=u16[:, dt * TC2:(dt + 1) * TC2],
                                       in1=Bv[n % 2][:], op=ALU.mult),
                                  wait=wB, inc=s_evV)
                if i > 0:
                    emit_tail(i - 1)
        emit_tail(NST * KTI - 1)
        if "dbg_y" in g:
            yw = [(yadd_sem, max(yadd_v.values())), (s_evV, s_evV.n)]
            P.dma("gpsimd", g["dbg_y"].rearrange("(kt p) t -> p kt t", p=128)[:, :, cs],
                  xc32[:, 0:KTI * TC2].rearrange("p (kt t) -> p kt t", kt=KTI),
                  s_st, wait=yw)
            P.dma("gpsimd", g["dbg_dt"].rearrange("(kt p) t -> p kt t", p=128)[:, :, cs],
                  dt16[:, 0:KTI * TC2].rearrange("p (kt t) -> p kt t", kt=KTI),
                  s_st, wait=[(s_evV, vdt[KTI - 1])])
        vxch = P.dma("scalar", xch32[:, 0:4096].rearrange("p (kt t) -> p kt t", kt=KTH),
                     src_r[:, :, cs], s_x, wait=[(s_evA, s_evA.n)])
        gat_v = {}
        sz_v = {}
        for dt in range(2):
            sz_v[dt] = P.dma("scalar", g["szst"][dt % 2][:],
                             sz[dt * 128:(dt + 1) * 128, cs], s_sc,
                             wait=[(s_evV, s_evV.n), (s_evP, s_evP.n)])
        for dt in range(KTI):
            gat_v[dt] = P.emit("vector", "tensor_tensor",
                               dict(out=u16[:, dt * TC2:(dt + 1) * TC2],
                                    in0=xc32[:, dt * TC2:(dt + 1) * TC2],
                                    in1=g["szst"][dt % 2][:], op=ALU.mult),
                               wait=[(yadd_sem, yadd_v[dt]), (s_sc, sz_v[dt])],
                               inc=s_evV)
            if dt + 2 < KTI:
                sz_v[dt + 2] = P.dma("scalar", g["szst"][(dt + 2) % 2][:],
                                     sz[(dt + 2) * 128:(dt + 3) * 128, cs], s_sc,
                                     wait=[(s_evV, gat_v[dt])])
        vgate_all = gat_v[KTI - 1]
        vev_last = None
        for oh in range(KTH):
            po = PS[4 + (oh % 2)]
            pairs = [(slot_wo[:, kt * 1024 + oh * 128: kt * 1024 + oh * 128 + 128],
                      u16[:, kt * TC2: kt * TC2 + 512]) for kt in range(KTI)]
            tO = mm_group(P, po[:], pairs,
                          wait=pp.wait(po) +
                               ([(s_w, g["wout_ready"]), (s_evV, vgate_all)]
                                if oh == 0 else []),
                          inc=s_pe)
            vev_last = P.emit("vector", "tensor_tensor",
                              dict(out=xch32[:, oh * TC2:(oh + 1) * TC2],
                                   in0=xch32[:, oh * TC2:(oh + 1) * TC2], in1=po[:],
                                   op=ALU.add),
                              wait=[(s_pe, tO), (s_x, vxch)], inc=s_evV)
            pp.free(po, s_evV, vev_last)
        P.dma("scalar", dst_r[:, :, cs],
              xch32[:, 0:4096].rearrange("p (kt t) -> p kt t", kt=KTH),
              s_st, wait=[(s_evV, vev_last)])
    P.wait("tensor", (s_pe, s_pe.n))
    P.emit("tensor", "nop", {}, inc=s_wfree)



"""Main assembly: build_program (all phases) + kernel(**inputs).
Appended after fw/glb/p1/p2 contents when generating kernel.py."""
H, T = 1024, 4096
NE = 8


def build_program(stage="full"):
    nc = bass.Bass("TRN2", target_bir_lowering=False)
    x_T = nc.dram_tensor("x_T", [H, T], F32, kind="ExternalInput").ap()
    out_T = nc.dram_tensor("out_T", [H, T], F32, kind="ExternalOutput").ap()
    xres1 = nc.dram_tensor("xres1", [H, T], F32, kind="Internal").ap()
    xres2 = nc.dram_tensor("xres2", [H, T], F32, kind="Internal").ap()
    f_in, wbf = declare_ffn_inputs(nc)
    from contextlib import ExitStack
    with ExitStack() as stack:
        P = Prog(nc, stack)
        g = alloc_globals(P)
        declare_p2(P, g, nc)
        emit_consts(P, g, nc)
        cast_order = ["router", "su", "sg", "sd"]
        for e in range(NE):
            cast_order += [f"up{e}", f"gate{e}", f"down{e}"]
        emit_ffn_setup(P, g, nc, f_in, wbf, cast_order)
        emit_p2_casts(P, g)
        emit_p2_prelude(P, g)
        g["wbf"] = wbf
        emit_ffn_phase(P, g, src_hbm=x_T,
                       dst_hbm=(out_T if stage == "p1" else xres1), n_experts=NE,
                       n_chunks=4, tc=1024)
        if stage != "p1":
            emit_p2a(P, g, src_hbm=xres1)
            emit_p2b(P, g, src_hbm=xres1,
                     dst_hbm=(out_T if stage == "p12" else xres2))
        if stage not in ("p1", "p12"):
            emit_ffn_phase(P, g, src_hbm=xres2, dst_hbm=out_T, n_experts=NE,
                           n_chunks=4, tc=1024)
        P.wait("gpsimd", (g["s_st"], g["s_st"].n))
        P.wait("sync", (g["s_st"], g["s_st"].n))
        P.wait("scalar", (g["s_st"], g["s_st"].n))
        block = stack.enter_context(nc.Block())
        P.finalize(block)
    return nc


_NC_CACHE = []


def kernel(L0_attn_x, L0_attn_in_proj_w, L0_attn_conv_w, L0_attn_conv_b,
           L0_attn_x_proj_w, L0_attn_dt_proj_w, L0_attn_dt_proj_b, L0_attn_A_log,
           L0_attn_D, L0_attn_out_proj_w, L0_ffn_shared_gate, L0_ffn_shared_up,
           L0_ffn_shared_down, L0_ffn_gate_w, L0_ffn_up_w, L0_ffn_down_w,
           L0_ffn_router_w, L0_ffn_routing_bias):
    import ml_dtypes
    if not _NC_CACHE:
        _NC_CACHE.append(build_program())
    nc = _NC_CACHE[0]
    f32 = lambda a: np.ascontiguousarray(np.asarray(a, dtype=np.float32))

    def lay(vec):
        return np.ascontiguousarray(np.asarray(vec, np.float32).reshape(16, 128).T)

    WT = f32(L0_attn_in_proj_w).T
    base = {
        "f_su": f32(np.asarray(L0_ffn_shared_up).T),
        "f_sg": f32(np.asarray(L0_ffn_shared_gate).T),
        "f_sd": f32(np.asarray(L0_ffn_shared_down).T),
        "f_router": f32(np.asarray(L0_ffn_router_w).T),
        "c_bias": f32(L0_ffn_routing_bias).reshape(1, 8),
        "c_sel8": np.stack([(np.arange(8)[:, None] == e) * np.ones((8, 128), np.float32)
                            for e in range(8)], axis=1).reshape(8, 1024
                           ).astype(ml_dtypes.bfloat16),
        "c_ident": np.eye(128, dtype=np.float32),
        "c_ones": np.ones((128, 1), np.float32),
        "c_ones_r": np.ones((1, 128), np.float32),
        "f_winx": f32(WT[:, :2048]),
        "f_winz": f32(WT[:, 2048:]),
        "f_wout": f32(np.asarray(L0_attn_out_proj_w).T),
        "f_xp": f32(np.asarray(L0_attn_x_proj_w).T),
        "c_dtw": lay(np.asarray(L0_attn_dt_proj_w)[:, 0]),
        "c_dtb": lay(L0_attn_dt_proj_b),
        "c_Dd": lay(L0_attn_D),
        "c_convb": lay(L0_attn_conv_b),
        "c_convw": np.ascontiguousarray(
            np.asarray(L0_attn_conv_w, np.float32)[:, 0, :]
            .reshape(16, 128, 4).transpose(1, 0, 2).reshape(128, 64)),
        "c_alog": np.ascontiguousarray(
            np.asarray(L0_attn_A_log, np.float32)
            .reshape(16, 128, 16).transpose(1, 0, 2).reshape(128, 256)),
    }
    for e in range(NE):
        base[f"f_up{e}"] = f32(np.asarray(L0_ffn_up_w)[e])
        base[f"f_gate{e}"] = f32(np.asarray(L0_ffn_gate_w)[e])
        base[f"f_down{e}"] = f32(np.asarray(L0_ffn_down_w)[e])
    x = np.asarray(L0_attn_x, np.float32)
    in_maps = [dict(base, x_T=np.ascontiguousarray(x[b].T)) for b in range(8)]
    res = run_bass_kernel_spmd(nc, in_maps, core_ids=list(range(8)))
    out = np.stack([np.asarray(res.results[b]["out_T"]).T for b in range(8)])
    return out.astype(np.float32)
